# version 18 (final)
# baseline (speedup 1.0000x reference)
import math
import os
from contextlib import ExitStack

import numpy as np
import ml_dtypes

import concourse.bass as bass
import concourse.mybir as mybir
from concourse.bass_utils import run_bass_kernel_spmd

F32 = mybir.dt.float32
BF16 = mybir.dt.bfloat16
AF = mybir.ActivationFunctionType
ALU = mybir.AluOpType
AX = mybir.AxisListType

NCORE = 8
R = 1024
L = 4096
NQ = 32
NKT = 17
KH = 2049
KHP = NKT * 128
RB = 256
NRB = R // RB
RTB = RB // 128
NRT = R // 128
EPS = 1e-5

_BF = ml_dtypes.bfloat16
_TABLES = None


def _make_tables():
    global _TABLES
    if _TABLES is not None:
        return _TABLES
    t = np.arange(L, dtype=np.float64)
    k = np.arange(KHP, dtype=np.float64)
    ang = 2.0 * np.pi / L
    kmask = (k <= 2048).astype(np.float64)
    wk = np.where((k == 0) | (k == 2048), 1.0, 2.0) * kmask

    th = np.arange(KHP, dtype=np.float64)
    wt = np.where((th == 0) | (th == 2048), 0.5, np.where(th > 2048, 0.0, 1.0))
    phase = np.outer(th, k) * ang
    t1c = np.cos(phase) * kmask[None, :] * wt[:, None]
    t1s = np.sin(phase) * kmask[None, :] * wt[:, None]

    def fwd_layout(m):
        return np.ascontiguousarray(
            m.reshape(NKT, 128, NKT, 128).transpose(2, 1, 0, 3)).astype(_BF)

    tau = np.arange(KHP, dtype=np.float64)
    taumask = (tau <= 2048).astype(np.float64)
    t2 = np.cos(np.outer(k, tau) * ang) * (wk / L)[:, None] * taumask[None, :]

    def half_layout(m):
        return np.ascontiguousarray(
            m.reshape(NKT, 128, NKT, 128).transpose(2, 1, 0, 3)).astype(_BF)

    d = np.arange(KHP, dtype=np.float64)
    wd = np.where((d == 0) | (d >= 2049), 0.0, np.where(d == 2048, 1.0, 2.0))
    t3 = np.cos(np.outer(d, k) * ang) * wd[:, None] * kmask[None, :]

    t4base = np.outer(k, th) * ang
    t4c = np.cos(t4base) * (wk / L)[:, None]
    t4s = np.sin(t4base) * (wk / L)[:, None]

    def inv_layout(m):
        return np.ascontiguousarray(
            m.reshape(NKT, 128, NKT, 128).transpose(2, 1, 0, 3)).astype(_BF)

    _out = dict(
        t1c=fwd_layout(t1c), t1s=fwd_layout(t1s),
        t2c=half_layout(t2), t3c=half_layout(t3),
        t4c=inv_layout(t4c), t4s=inv_layout(t4s),
        identb=np.asarray(np.eye(128, dtype=np.float32).astype(_BF)),
        revb=np.asarray(np.fliplr(np.eye(128)).astype(np.float32).astype(_BF)),
        identf=np.eye(128, dtype=np.float32),
        onesb=np.asarray(np.ones((1, 128), np.float32).astype(_BF)),
        onesf=np.ones((1, 128), np.float32),
    )
    globals()["_TABLES"] = _out
    return _out


class Rec:

    def __init__(self):
        self.steps = []
        self.cnt = {}
        self.disabled = False

    def now(self, sem):
        return (sem, self.cnt.get(sem, 0))

    def barrier(self):
        return [self.now(s) for s in self.cnt]

    def step(self, eng, fn, waits=(), inc=None, n=1):
        w = {}
        for tk in waits:
            if tk is None:
                continue
            s, v = tk
            if v > 0:
                w[s] = max(w.get(s, 0), v)
        tok = None
        if inc is not None:
            self.cnt[inc] = self.cnt.get(inc, 0) + n
            tok = (inc, self.cnt[inc])
        if not self.disabled:
            self.steps.append((eng, fn, sorted(w.items()), inc, n))
        return tok


def build_program():
    nc = bass.Bass("TRN2", target_bir_lowering=False)
    tb = _make_tables()

    d_x = nc.dram_tensor("x", [R, L], F32, kind="ExternalInput").ap()
    d_xT = nc.dram_tensor("xT", [L, R], F32, kind="ExternalInput").ap()
    d_xTr = nc.dram_tensor("xTr", [L, R], F32, kind="ExternalInput").ap()
    d_w = nc.dram_tensor("w", [L, L], F32, kind="ExternalInput").ap()
    d_bias = nc.dram_tensor("biasr", [1, L], F32, kind="ExternalInput").ap()
    d_gate = nc.dram_tensor("gateb", [128, 16], F32, kind="ExternalInput").ap()
    d_lA = nc.dram_tensor("lA", [16, 128, 1024], F32, kind="ExternalInput").ap()
    d_lAs = nc.dram_tensor("lAs", [128, 1024], F32, kind="ExternalInput").ap()
    d_lB = nc.dram_tensor("lB", [16, 32, L], F32, kind="ExternalInput").ap()
    d_lBs = nc.dram_tensor("lBs", [32, L], F32, kind="ExternalInput").ap()
    d_t1c = nc.dram_tensor("t1c", list(tb["t1c"].shape), BF16, kind="ExternalInput").ap()
    d_t1s = nc.dram_tensor("t1s", list(tb["t1s"].shape), BF16, kind="ExternalInput").ap()
    d_t2c = nc.dram_tensor("t2c", list(tb["t2c"].shape), BF16, kind="ExternalInput").ap()
    d_t3c = nc.dram_tensor("t3c", list(tb["t3c"].shape), BF16, kind="ExternalInput").ap()
    d_t4c = nc.dram_tensor("t4c", list(tb["t4c"].shape), BF16, kind="ExternalInput").ap()
    d_t4s = nc.dram_tensor("t4s", list(tb["t4s"].shape), BF16, kind="ExternalInput").ap()
    d_idb = nc.dram_tensor("identb", [128, 128], BF16, kind="ExternalInput").ap()
    d_rev = nc.dram_tensor("revb", [128, 128], BF16, kind="ExternalInput").ap()
    d_idf = nc.dram_tensor("identf", [128, 128], F32, kind="ExternalInput").ap()
    d_onb = nc.dram_tensor("onesb", [1, 128], BF16, kind="ExternalInput").ap()
    d_onf = nc.dram_tensor("onesf", [1, 128], F32, kind="ExternalInput").ap()

    d_out = nc.dram_tensor("out", [R, L], F32, kind="ExternalOutput").ap()
    d_yscr = nc.dram_tensor("y_scr", [NRT, 128, L], F32).ap()
    d_zscr = nc.dram_tensor("z_scr", [NQ, 128, R], BF16).ap()
    d_fscr = nc.dram_tensor("f_scr", [NRT, 128, L], F32).ap()

    rec = Rec()

    def dma(eng, sem, out, in_, waits=()):
        return rec.step(eng, lambda e, o=out, i=in_: e.dma_start(out=o, in_=i),
                        waits=waits, inc=sem, n=16)

    def tab2(sem, dst, src, waits=()):
        t1 = dma("sync", sem, dst[:, 0:KHP // 2], src[:, 0:KHP // 2],
                 waits=waits)
        t2 = dma("sync", sem + "b", dst[:, KHP // 2:KHP], src[:, KHP // 2:KHP],
                 waits=waits)
        return [t1, t2]

    def pe(fn, waits=()):
        return rec.step("tensor", fn, waits=waits, inc="pe")

    def ve(fn, waits=()):
        return rec.step("vector", fn, waits=waits, inc="ve")

    def se(fn, waits=()):
        return rec.step("scalar", fn, waits=waits, inc="se")

    with ExitStack() as ctx:
        ee = ctx.enter_context

        s_idb = ee(nc.sbuf_tensor("s_idb", [128, 128], BF16))
        s_rev = ee(nc.sbuf_tensor("s_rev", [128, 128], BF16))
        s_idf = ee(nc.sbuf_tensor("s_idf", [128, 128], F32))
        s_onb = ee(nc.sbuf_tensor("s_onb", [1, 128], BF16))
        s_onf = ee(nc.sbuf_tensor("s_onf", [1, 128], F32))
        s_gate = ee(nc.sbuf_tensor("s_gate", [128, 16], F32))
        stat1 = ee(nc.sbuf_tensor("stat1", [128, 64], F32))
        stat2 = ee(nc.sbuf_tensor("stat2", [128, 64], F32))
        stat3 = ee(nc.sbuf_tensor("stat3", [128, 64], F32))
        stat4 = ee(nc.sbuf_tensor("stat4", [128, 64], F32))
        fsum = ee(nc.sbuf_tensor("fsum", [128, 64], F32))
        fsq = ee(nc.sbuf_tensor("fsq", [128, 64], F32))
        mcol = ee(nc.sbuf_tensor("mcol", [128, 16], F32))
        m8ab = ee(nc.sbuf_tensor("m8ab", [128, 16], F32))
        m8all = ee(nc.sbuf_tensor("m8all", [128, 32], F32))
        rstdrow = ee(nc.sbuf_tensor("rstdrow", [1, R], F32))
        a_comb = ee(nc.sbuf_tensor("a_comb", [128, 2048], BF16))
        b_comb = ee(nc.sbuf_tensor("b_comb", [64, L], BF16))
        s_ax = ee(nc.sbuf_tensor("s_ax", [64, 1024], BF16))
        biasbf = ee(nc.sbuf_tensor("biasbf", [1, L], BF16))

        ar45 = ee(nc.sbuf_tensor("ar45", [128, 4 * NKT * RB], BF16))
        u_r = ee(nc.sbuf_tensor("u_r", [128, NKT * RB], BF16))
        u_i = ee(nc.sbuf_tensor("u_i", [128, NKT * RB], BF16))
        slabP = ee(nc.sbuf_tensor("slabP", [128, NKT * RB], BF16))
        ctok = ee(nc.sbuf_tensor("ctok", [128, RTB * KHP], BF16))
        etok = ee(nc.sbuf_tensor("etok", [128, RTB * KHP], BF16))
        tabA = ee(nc.sbuf_tensor("tabA", [128, 2 * KHP], BF16))
        tabB = ee(nc.sbuf_tensor("tabB", [128, 2 * KHP], BF16))
        stgf = ee(nc.sbuf_tensor("stgf", [128, 512], F32))
        bcst = ee(nc.sbuf_tensor("bcst", [128, RB], F32))
        stgc = ee(nc.sbuf_tensor("stgc", [128, 512], BF16))
        asmA = ee(nc.sbuf_tensor("asmA", [128, L], F32))
        asmB = ee(nc.sbuf_tensor("asmB", [128, L], F32))
        asmC = ee(nc.sbuf_tensor("asmC", [128, L], F32))

        LH = L // 2
        lb_guard = nc.sbuf_tensor("lb_st", [32, 2 * LH], F32)
        lb_st = lb_guard.__enter__()
        be_guard = nc.sbuf_tensor("beff", [32, 2 * LH], F32)
        beff = be_guard.__enter__()
        lb_v = [lb_st[:, 0:LH], lb_st[:, LH:2 * LH]]
        be_v = [beff[:, 0:LH], beff[:, LH:2 * LH]]
        be_guard.__exit__(None, None, None)
        lb_guard.__exit__(None, None, None)

        vrtok = ee(nc.sbuf_tensor("vrtok", [128, RTB * L], BF16))
        ztok = ee(nc.sbuf_tensor("ztok", [128, L], BF16))
        zstg = ee(nc.sbuf_tensor("zstg", [128, L], BF16))
        trash = vrtok[:, 0:L]

        HC = NKT * RB
        xt_v = ar45[:, 0:HC]
        xtr_v = ar45[:, HC:2 * HC]
        ue_v = ar45[:, 2 * HC:3 * HC]
        uo_v = ar45[:, 3 * HC:4 * HC]
        v_r = ar45[:, 0:NKT * RB]
        v_i = ar45[:, NKT * RB:2 * NKT * RB]
        tabAv = [tabA[:, 0:KHP], tabA[:, KHP:2 * KHP]]
        tabB1 = tabB[:, 0:KHP]
        tabB4 = [tabB[:, 0:KHP], tabB[:, KHP:2 * KHP]]
        stgm = tabB[:, 0:2 * KHP]
        sq1 = stgf[:, 0:RB]
        sq2 = stgf[:, RB:2 * RB]
        la_v = [asmC[:, 0:1024], asmC[:, 1024:2048]]
        ae_v = [asmC[:, 2048:3072], asmC[:, 3072:4096]]
        xv = [asmA[:], asmB[:]]
        wv = [ztok[:, i * 512:(i + 1) * 512] for i in range(4)]
        zv2 = [zstg[:, i * 1024:(i + 1) * 1024] for i in range(4)]
        fstg = stgf[:, 0:512]
        tr512 = trash[:, 0:512]

        sem_names = ["pe", "ve", "se", "dtab0", "dtab1", "dtabS0", "dtabS1",
                     "dtab0b", "dtab1b", "dtabS0b", "dtabS1b",
                     "dx0", "dx1", "dxt",
                     "dwh0", "dwh1", "dwh2", "dwh3",
                     "dwh4", "dwh5", "dwh6", "dwh7",
                     "dz0", "dz1", "dz2", "dz3", "dzw",
                     "dyw", "dyr", "dfw0", "dfw1", "dfw2", "dfw3", "dfr",
                     "dout0", "dout1", "dmisc",
                     "dlA0", "dlA1", "dlB0", "dlB1", "dbias"]
        sems = {s: ee(nc.semaphore(s)) for s in sem_names}

        dma("sync", "dmisc", s_idb[:], d_idb)
        dma("sync", "dmisc", s_rev[:], d_rev)
        dma("sync", "dmisc", s_idf[:], d_idf)
        dma("sync", "dmisc", s_onb[:], d_onb)
        dma("sync", "dmisc", s_onf[:], d_onf)
        tok_misc = dma("sync", "dmisc", s_gate[:], d_gate)

        lt = {}

        def lora_gen():
            tok_bc = None
            tBs_t = None
            for h in range(2):
                hs = slice(h * LH, (h + 1) * LH)
                tB = [None, None]
                last_b = None
                pb_use = [tok_bc, tok_bc]
                for n in range(16):
                    p = n % 2
                    tB[p] = dma("sync", "dlB%d" % p, lb_v[p], d_lB[n][:, hs],
                                waits=[pb_use[p]])
                    gB = s_gate[0:32, n:n + 1]
                    dst_b = be_v[(n + 1) % 2]
                    src_b = be_v[n % 2]
                    if n == 0:
                        last_b = ve(lambda e, d0=dst_b, s0=lb_v[p], g=gB:
                                    e.tensor_scalar(d0, s0, g, None,
                                                    op0=ALU.mult),
                                    waits=[tB[p], tok_misc, tok_bc])
                    else:
                        last_b = ve(lambda e, d0=dst_b, s0=lb_v[p], g=gB,
                                    acc=src_b:
                                    e.scalar_tensor_tensor(d0, s0, g, acc,
                                                           op0=ALU.mult,
                                                           op1=ALU.add),
                                    waits=[tB[p], last_b])
                    pb_use[p] = last_b
                    yield
                be_fin = be_v[0]
                t_bc0 = ve(lambda e, hs=hs, be_fin=be_fin: e.tensor_copy(
                    b_comb[0:32, hs], be_fin), waits=[last_b])
                tBs_t = dma("sync", "dlB1", lb_v[1], d_lBs[:, hs],
                            waits=[last_b, tBs_t])
                tok_bc = ve(lambda e, hs=hs: e.tensor_copy(
                    b_comb[32:64, hs], lb_v[1]), waits=[tBs_t, t_bc0])
                yield
            lt["bcomb"] = tok_bc
            tA = [None, None]
            last_a = None
            pa_use = [None, None]
            la_ld = [asmC[:, 0:1024], asmC[:, 1024:2048]]
            for n in range(16):
                p = n % 2
                tA[p] = dma("sync", "dlA%d" % p, la_ld[p], d_lA[n],
                            waits=[pa_use[p]])
                gA = s_gate[:, n:n + 1]
                dst_a = ae_v[(n + 1) % 2]
                src_a = ae_v[n % 2]
                if n == 0:
                    last_a = ve(lambda e, d0=dst_a, s0=la_ld[p], g=gA:
                                e.tensor_scalar(d0, s0, g, None, op0=ALU.mult),
                                waits=[tA[p], tok_misc])
                else:
                    last_a = ve(lambda e, d0=dst_a, s0=la_ld[p], g=gA, acc=src_a:
                                e.scalar_tensor_tensor(d0, s0, g, acc,
                                                       op0=ALU.mult,
                                                       op1=ALU.add),
                                waits=[tA[p], last_a])
                pa_use[p] = last_a
                yield
            ae_fin = ae_v[0]
            ac3 = a_comb.ap().rearrange("p (q r) -> p q r", r=64)
            ae3 = ae_fin.rearrange("p (q r) -> p q r", r=32)
            t_ac0 = ve(lambda e: e.tensor_copy(ac3[:, :, 0:32], ae3),
                       waits=[last_a])
            tAs = dma("sync", "dlA1", la_ld[1], d_lAs, waits=[last_a])
            las3 = la_ld[1].rearrange("p (q r) -> p q r", r=32)
            lt["acomb"] = ve(lambda e: e.tensor_copy(ac3[:, :, 32:64], las3),
                             waits=[tAs, t_ac0])
            yield

        def drain(gen, k):
            if gen is None:
                return True
            for _ in range(k):
                try:
                    next(gen)
                except StopIteration:
                    return True
            return False

        def ln_stats(stat, base, tsum, tsq):
            c = lambda j: stat[:, base + j:base + j + 1]
            t1 = ve(lambda e: e.tensor_scalar(c(2), c(0), 1.0 / L, None, op0=ALU.mult),
                    waits=[tsum])
            t2 = ve(lambda e: e.tensor_scalar(c(3), c(1), 1.0 / L, None, op0=ALU.mult),
                    waits=[tsq])
            t3 = ve(lambda e: e.tensor_tensor(c(4), c(2), c(2), op=ALU.mult), waits=[t1])
            t4 = ve(lambda e: e.tensor_tensor(c(4), c(3), c(4), op=ALU.subtract),
                    waits=[t2, t3])
            t5 = ve(lambda e: e.tensor_scalar(c(4), c(4), EPS, None, op0=ALU.add),
                    waits=[t4])
            t6 = ve(lambda e: e.reciprocal(c(3), c(4)), waits=[t5])
            t7 = se(lambda e: e.activation(c(5), c(3), AF.Sqrt), waits=[t6])
            t8 = ve(lambda e: e.tensor_scalar(c(6), c(2), -1.0, c(5),
                                              op0=ALU.mult, op1=ALU.mult), waits=[t7])
            return t8

        with nc.psum_tensor("pmA", [128, RB], F32) as pmA, \
             nc.psum_tensor("pmB", [128, RB], F32) as pmB, \
             nc.psum_tensor("pmC", [128, RB], F32) as pmC, \
             nc.psum_tensor("pmD", [128, RB], F32) as pmD, \
             nc.psum_tensor("ptrA", [128, 512], BF16) as ptrA, \
             nc.psum_tensor("ptrB", [128, 512], BF16) as ptrB, \
             nc.psum_tensor("psB", [128, 512], F32) as psB:

            pr = pmA[:]
            pi = pmB[:]
            prv = [pmA[:], pmC[:]]
            piv = [pmB[:], pmD[:]]
            ptrv = [ptrA[:], ptrB[:]]

            tok_stats = [None] * NRT
            tok_rrow = [None] * NRT
            tld = [None, None]
            tuse = [None, None]
            tok_psB = None
            for rt in range(NRT):
                p = rt % 2
                tld[p] = dma("sync", "dx%d" % p, xv[p],
                             d_x[rt * 128:(rt + 1) * 128, :],
                             waits=[tuse[p]] if tuse[p] else [])
                b8 = rt * 8
                ts1 = ve(lambda e, s0=xv[p], o=stat1[:, b8:b8 + 1]:
                         e.reduce_sum(o, s0, axis=AX.X), waits=[tld[p]])
                ts2 = se(lambda e, s0=xv[p], o=stat1[:, b8 + 1:b8 + 2]:
                         e.activation(etok[:, 0:L], s0, AF.Square, accum_out=o),
                         waits=[tld[p], ts1])
                tuse[p] = ts2
                tok_stats[rt] = ln_stats(stat1, b8, ts1, ts2)
                tt = pe(lambda e, c=stat1[:, b8 + 5:b8 + 6]:
                        e.transpose(psB[0:1, 0:128], c, s_idf[:]),
                        waits=[tok_stats[rt], tok_misc, tok_psB])
                tok_rrow[rt] = ve(lambda e, rt=rt: e.tensor_copy(
                    rstdrow[0:1, rt * 128:(rt + 1) * 128], psB[0:1, 0:128]),
                    waits=[tt])
                tok_psB = tok_rrow[rt]
            tok_allstats = [t for t in tok_stats + tok_rrow]

            tok_prev_block = None
            tok_zwr = []
            tok_ywr = []
            tok_ptr_free = [None, None]

            ast = {"A": None, "B": None, "C": [], "zt": None, "vr": []}

            def asm_gen(b, tok_vr_done, azq):
                for rq in range(RTB):
                    rt = b * RTB + rq
                    p = rt % 2
                    b8 = rt * 8
                    m1 = stat1[:, b8 + 2:b8 + 3]
                    r1 = stat1[:, b8 + 5:b8 + 6]
                    tx = dma("sync", "dx%d" % p, asmA[:],
                             d_x[rt * 128:(rt + 1) * 128, :],
                             waits=[ast["A"]] if ast["A"] else tok_allstats)
                    yield
                    a1 = ve(lambda e, m1=m1, r1=r1: e.tensor_scalar(
                        asmB[:], asmA[:], m1, r1,
                        op0=ALU.subtract, op1=ALU.mult),
                        waits=[tx] + ([ast["B"]] if ast["B"] else tok_allstats))
                    yield
                    a2 = ve(lambda e, rq=rq: e.tensor_tensor(
                        asmC[:], asmB[:], vrtok[:, rq * L:(rq + 1) * L],
                        op=ALU.add),
                        waits=[a1, tok_vr_done] + list(ast["C"]))
                    yield
                    s1 = ve(lambda e, o=stat2[:, b8:b8 + 1]:
                            e.reduce_sum(o, asmC[:], axis=AX.X), waits=[a2])
                    yield
                    s2 = se(lambda e, o=stat2[:, b8 + 1:b8 + 2]:
                            e.activation(trash[:], asmC[:], AF.Square,
                                         accum_out=o), waits=[a2, s1])
                    yield
                    tst = ln_stats(stat2, b8, s1, s2)
                    yield
                    a3 = ve(lambda e, m2=stat2[:, b8 + 2:b8 + 3],
                            r2=stat2[:, b8 + 5:b8 + 6]: e.tensor_scalar(
                        asmB[:], asmC[:], m2, r2,
                        op0=ALU.subtract, op1=ALU.mult), waits=[tst, s2, a2])
                    yield
                    a4 = ve(lambda e: e.tensor_tensor(
                        asmC[:], asmB[:], asmA[:], op=ALU.add), waits=[a3])
                    yield
                    tok_ywr.append(dma("sync", "dyw", d_yscr[rt], asmC[:],
                                       waits=[a4]))
                    yield
                    s3 = ve(lambda e, o=stat3[:, b8:b8 + 1]:
                            e.reduce_sum(o, asmC[:], axis=AX.X), waits=[a4])
                    yield
                    s4 = se(lambda e, o=stat3[:, b8 + 1:b8 + 2]:
                            e.activation(trash[:], asmC[:], AF.Square,
                                         accum_out=o), waits=[a4, s3])
                    yield
                    tst3 = ln_stats(stat3, b8, s3, s4)
                    yield
                    az = se(lambda e, nm=stat3[:, b8 + 6:b8 + 7],
                            r3=stat3[:, b8 + 5:b8 + 6]: e.activation(
                        ztok[:], asmC[:], AF.Identity, bias=nm, scale=r3),
                        waits=[tst3, s4] + ([ast["zt"]] if ast["zt"]
                                            else list(tok_lora)))
                    azq[rq] = az
                    ast["A"] = a4
                    ast["B"] = a4
                    ast["C"] = [tok_ywr[-1], s3, s4, az]
                    ast["vr"] = [a2, s4]
                    yield

            def emit_zturn(bz, rq, az):
                rt = bz * RTB + rq
                ev = None
                ttr = None
                for qg in range(8):
                    pz = qg % 2
                    for j in range(4):
                        q = qg * 4 + j
                        ttr = pe(lambda e, q=q, j=j, pz=pz: e.transpose(
                            ptrv[pz][:, j * 128:(j + 1) * 128],
                            ztok[:, q * 128:(q + 1) * 128], s_idb[:]),
                            waits=[az, tok_ptr_free[pz]] if j == 0 else [az])
                    ev = ve(lambda e, qg=qg, pz=pz: e.tensor_copy(
                        zstg[:, qg * 512:(qg + 1) * 512],
                        ptrv[pz][:, 0:512]),
                        waits=[ttr, tok_zwr[-1] if (qg == 0 and tok_zwr)
                               else None]
                        + (tok_lora if qg == 0 and not tok_zwr else []))
                    tok_ptr_free[pz] = ev
                zdst = d_zscr[:, :, rt * 128:(rt + 1) * 128].transpose([1, 0, 2])
                zsrc = zstg.ap().rearrange("p (q r) -> p q r", r=128)
                tok_zwr.append(dma("sync", "dzw", zdst, zsrc, waits=[ev]))
                ast["zt"] = ttr

            prev_entry = None
            lgen = lora_gen()
            pend_asm = None
            pend_az = None
            pend_b = None
            tok_psB_carry = tok_psB
            tok_lora = []
            tok_acomb = None
            tok_bcomb = None
            for b in range(NRB):
                rows = slice(b * RB, (b + 1) * RB)
                entry = [] if b == 0 else list(prev_entry)

                t_xt0 = dma("gpsimd", "dxt",
                            xt_v.rearrange("p (q r) -> p q r", r=RB),
                            d_xT[0:KHP, rows].rearrange("(q p) r -> p q r",
                                                        p=128),
                            waits=entry)
                t_xt = dma("gpsimd", "dxt",
                           xtr_v.rearrange("p (q r) -> p q r", r=RB),
                           d_xTr[0:KHP, rows].rearrange("(q p) r -> p q r",
                                                        p=128),
                           waits=[])
                t_ue = ve(lambda e: e.tensor_tensor(ue_v, xt_v, xtr_v, op=ALU.add),
                          waits=[t_xt] + entry)
                t_uo = ve(lambda e: e.tensor_tensor(uo_v, xt_v, xtr_v,
                                                    op=ALU.subtract),
                          waits=[t_xt, t_ue])
                t_k1 = pe(lambda e, rows=rows: e.matmul(
                    psB[:, 0:RB], s_onf[:], rstdrow[0:1, rows],
                    start=True, stop=True),
                    waits=[tok_rrow[2 * b], tok_rrow[2 * b + 1], tok_misc,
                           tok_psB_carry])
                t_rb = ve(lambda e: e.tensor_copy(bcst[:], psB[:, 0:RB]),
                          waits=[t_k1])

                tok_tab = [None, None]
                tok_tabu = [None, None]
                tok_sinuse = None
                tok_psf = [None, None]
                tok_u_last = None
                tok_P_last = None
                for kt in range(NKT):
                    p = kt % 2
                    tok_tab[p] = tab2("dtab%d" % p, tabAv[p],
                                      d_t1c[kt].rearrange("p q k -> p (q k)"),
                                      waits=[tok_tabu[p]] if tok_tabu[p]
                                      else entry)
                    tokb = tab2("dtabS0", tabB1,
                                d_t1s[kt].rearrange("p q k -> p (q k)"),
                                waits=[tok_sinuse] if tok_sinuse else entry)
                    mm = None
                    for q in range(NKT):
                        mm = pe(lambda e, q=q, p=p: e.matmul(
                            prv[p], tabAv[p][:, q * 128:(q + 1) * 128],
                            ue_v[:, q * RB:(q + 1) * RB],
                            start=(q == 0), stop=(q == NKT - 1)),
                            waits=tok_tab[p] + [t_ue, tok_psf[p]]
                            if q == 0 else [])
                    sl = slice(kt * RB, (kt + 1) * RB)
                    e1 = ve(lambda e, sl=sl, p=p: e.tensor_tensor(
                        u_r[:, sl], prv[p], bcst[:], op=ALU.mult), waits=[mm, t_rb])
                    for q in range(NKT):
                        mm = pe(lambda e, q=q, p=p: e.matmul(
                            piv[p], tabB1[:, q * 128:(q + 1) * 128],
                            uo_v[:, q * RB:(q + 1) * RB],
                            start=(q == 0), stop=(q == NKT - 1)),
                            waits=tokb + [t_uo, tok_psf[p]] if q == 0 else [])
                    tok_tabu[p] = mm
                    tok_sinuse = mm
                    e2 = ve(lambda e, sl=sl, p=p: e.tensor_tensor(
                        u_i[:, sl], piv[p], bcst[:], op=ALU.mult), waits=[mm, e1])
                    q1 = se(lambda e, sl=sl: e.activation(
                        sq1, u_r[:, sl], AF.Square), waits=[e1, tok_P_last])
                    q2 = se(lambda e, sl=sl: e.activation(
                        sq2, u_i[:, sl], AF.Square), waits=[e2, tok_P_last])
                    e3 = ve(lambda e, sl=sl: e.tensor_tensor(
                        slabP[:, sl], sq1, sq2, op=ALU.add), waits=[q1, q2])
                    if kt == 0:
                        e3 = ve(lambda e: e.memzero(slabP[0:1, 0:RB]), waits=[e3])
                    tok_psf[p] = e2
                    tok_u_last = e2
                    tok_P_last = e3
                    if b == 0:
                        drain(lgen, 1)
                    else:
                        drain(pend_asm, 1)

                if pend_asm is not None:
                    while pend_az[0] is None:
                        if drain(pend_asm, 1):
                            break
                    emit_zturn(pend_b, 0, pend_az[0])

                tok_tab = [None, None]
                tok_stgc_free = [None, None]
                tok_ct_last = None
                stgcv = [stgc[:, 0:RB], stgc[:, RB:2 * RB]]
                for tt_ in range(NKT):
                    p = tt_ % 2
                    tok_tab[p] = tab2("dtab%d" % p, tabAv[p][:, 0:KHP],
                                      d_t2c[tt_].rearrange("p q k -> p (q k)"),
                                      waits=[tok_tabu[p]])
                    mm = None
                    for kq in range(NKT):
                        mm = pe(lambda e, kq=kq, p=p: e.matmul(
                            prv[p], tabAv[p][:, kq * 128:(kq + 1) * 128],
                            slabP[:, kq * RB:(kq + 1) * RB],
                            start=(kq == 0), stop=(kq == NKT - 1)),
                            waits=tok_tab[p] + [tok_P_last, tok_psf[p]]
                            if kq == 0 else [])
                    tok_tabu[p] = mm
                    ec = ve(lambda e, p=p: e.tensor_copy(stgcv[p], prv[p]),
                            waits=[mm, tok_stgc_free[p]])
                    tok_psf[p] = ec
                    ttr = None
                    for rq in range(RTB):
                        ttr = pe(lambda e, rq=rq, p=p: e.transpose(
                            ptrv[p][:, rq * 128:(rq + 1) * 128],
                            stgcv[p][:, rq * 128:(rq + 1) * 128], s_idb[:]),
                            waits=[ec, tok_ptr_free[p]] if rq == 0 else [ec])
                    tok_stgc_free[p] = ttr
                    ev = None
                    for rq in range(RTB):
                        ev = ve(lambda e, rq=rq, tt_=tt_, p=p: e.tensor_copy(
                            ctok[:, rq * KHP + tt_ * 128:
                                 rq * KHP + (tt_ + 1) * 128],
                            ptrv[p][:, rq * 128:(rq + 1) * 128]), waits=[ttr])
                    tok_ptr_free[p] = ev
                    tok_ct_last = ev
                    if b == 0:
                        drain(lgen, 1)
                    else:
                        drain(pend_asm, 1)

                if pend_asm is not None:
                    while not drain(pend_asm, 4):
                        pass
                    emit_zturn(pend_b, 1, pend_az[1])
                    pend_asm = None
                if b == 0:
                    while not drain(lgen, 4):
                        pass
                    tok_acomb = lt["acomb"]
                    tok_bcomb = lt["bcomb"]
                    tok_lora = [tok_acomb, tok_bcomb]
                    ast["C"] = [tok_acomb]

                tok_et_last = None
                t6_last = None
                for rq in range(RTB):
                    cv = ctok[:, rq * KHP:(rq + 1) * KHP]
                    evv = etok[:, rq * KHP:(rq + 1) * KHP]
                    t1_ = ve(lambda e, cv=cv: e.max(m8ab[:, 0:8], cv[:, 0:KH]),
                             waits=[tok_ct_last, tok_tabu[0], tok_tabu[1]])
                    t2_ = ve(lambda e, cv=cv: e.max(m8ab[:, 8:16], cv[:, 1:2048]),
                             waits=[t1_])
                    t3_ = ve(lambda e, rq=rq: e.max(
                        m8all[:, rq * 8:(rq + 1) * 8], m8ab[:]), waits=[t2_])
                    c0 = m8all[:, rq * 8:rq * 8 + 1]
                    th = m8all[:, rq * 8 + 7:rq * 8 + 8]
                    t4_ = ve(lambda e, c0=c0, rq=rq: e.tensor_scalar(
                        mcol[:, rq:rq + 1], c0, -1.0, None, op0=ALU.mult),
                        waits=[t3_])
                    t5_ = ve(lambda e, cv=cv, th=th: e.tensor_scalar(
                        stgm[:, 0:KHP], cv, th, None, op0=ALU.is_ge),
                        waits=[t3_])
                    t6_ = se(lambda e, cv=cv, rq=rq: e.activation(
                        stgm[:, KHP:2 * KHP], cv, AF.Exp,
                        bias=mcol[:, rq:rq + 1]), waits=[t4_])
                    t6_last = t6_
                    tok_et_last = ve(lambda e, evv=evv: e.tensor_tensor(
                        evv, stgm[:, KHP:2 * KHP], stgm[:, 0:KHP],
                        op=ALU.mult), waits=[t6_, t5_]
                        + (tok_allstats if b == 0 else []))

                efm = ctok
                tok_efm = None
                for dt_ in range(NKT):
                    p = dt_ % 2
                    ttr = None
                    for rq in range(RTB):
                        ttr = pe(lambda e, rq=rq, dt_=dt_, p=p: e.transpose(
                            ptrv[p][:, rq * 128:(rq + 1) * 128],
                            etok[:, rq * KHP + dt_ * 128:
                                 rq * KHP + (dt_ + 1) * 128], s_idb[:]),
                            waits=[tok_et_last, tok_ptr_free[p]] if rq == 0 else [])
                    ev = None
                    for rq in range(RTB):
                        ev = ve(lambda e, rq=rq, dt_=dt_, p=p: e.tensor_copy(
                            efm[:, dt_ * RB + rq * 128:
                                dt_ * RB + (rq + 1) * 128],
                            ptrv[p][:, rq * 128:(rq + 1) * 128]), waits=[ttr])
                    tok_ptr_free[p] = ev
                    tok_efm = ev

                tok_tab = [None, None]
                tok_v_last = None
                for kt in range(NKT):
                    p = kt % 2
                    tok_tab[p] = tab2("dtab%d" % p, tabAv[p][:, 0:KHP],
                                      d_t3c[kt].rearrange("p q k -> p (q k)"),
                                      waits=[tok_tabu[p]])
                    mm = None
                    for dq in range(NKT):
                        mm = pe(lambda e, dq=dq, p=p: e.matmul(
                            prv[p], tabAv[p][:, dq * 128:(dq + 1) * 128],
                            efm[:, dq * RB:(dq + 1) * RB],
                            start=(dq == 0), stop=(dq == NKT - 1)),
                            waits=tok_tab[p] + [tok_efm, tok_psf[p]]
                            if dq == 0 else [])
                    tok_tabu[p] = mm
                    sl = slice(kt * RB, (kt + 1) * RB)
                    x1 = ve(lambda e, sl=sl, p=p: e.tensor_tensor(
                        v_r[:, sl], prv[p], u_r[:, sl], op=ALU.mult), waits=[mm])
                    x2 = ve(lambda e, sl=sl, p=p: e.tensor_tensor(
                        v_i[:, sl], prv[p], u_i[:, sl], op=ALU.mult), waits=[x1])
                    tok_psf[p] = x2
                    tok_v_last = x2

                tok_tab = [None, None]
                tok_vr_last = None
                tok_psB4 = [t_rb, t_rb]
                t4stg = [[etok[:, (3 * p + j) * RB:(3 * p + j + 1) * RB]
                          for j in range(3)] for p in range(2)]
                tok_t4stg = [None, None]
                for tt_ in range(NKT):
                    p = tt_ % 2
                    tok_tab[p] = tab2("dtab%d" % p, tabAv[p][:, 0:KHP],
                                      d_t4c[tt_].rearrange("p q k -> p (q k)"),
                                      waits=[tok_tabu[p]])
                    tokb = tab2("dtabS%d" % p, tabB4[p],
                                d_t4s[tt_].rearrange("p q k -> p (q k)"),
                                waits=[tok_tabu[p],
                                       tok_et_last if tt_ < 2 else None,
                                       tok_sinuse if tt_ < 2 else None])
                    mm = None
                    for kq in range(NKT):
                        mm = pe(lambda e, kq=kq, p=p: e.matmul(
                            prv[p], tabAv[p][:, kq * 128:(kq + 1) * 128],
                            v_r[:, kq * RB:(kq + 1) * RB],
                            start=(kq == 0), stop=(kq == NKT - 1)),
                            waits=tok_tab[p] + [tok_v_last, tok_psf[p]]
                            if kq == 0 else [])
                    mmb = None
                    for kq in range(NKT):
                        mmb = pe(lambda e, kq=kq, p=p: e.matmul(
                            piv[p], tabB4[p][:, kq * 128:(kq + 1) * 128],
                            v_i[:, kq * RB:(kq + 1) * RB],
                            start=(kq == 0), stop=(kq == NKT - 1)),
                            waits=tokb + [tok_psf[p]] if kq == 0 else [])
                    tok_tabu[p] = mmb
                    vre, vpl, vmn = t4stg[p]
                    ec = ve(lambda e, p=p, vre=vre: e.tensor_copy(vre, prv[p]),
                            waits=[mm, tok_t4stg[p],
                                   tok_efm if tt_ < 2 else None])
                    epl = ve(lambda e, p=p, vre=vre, vpl=vpl: e.tensor_tensor(
                        vpl, vre, piv[p], op=ALU.add), waits=[ec, mmb])
                    emn = ve(lambda e, p=p, vre=vre, vmn=vmn: e.tensor_tensor(
                        vmn, vre, piv[p], op=ALU.subtract), waits=[epl])
                    tok_psf[p] = emn
                    ttr = None
                    for rq in range(RTB):
                        ttr = pe(lambda e, rq=rq, p=p, vpl=vpl: e.transpose(
                            ptrv[p][:, rq * 128:(rq + 1) * 128],
                            vpl[:, rq * 128:(rq + 1) * 128], s_idb[:]),
                            waits=[epl, tok_ptr_free[p]] if rq == 0 else [epl])
                    ev = None
                    ncols = 128 if tt_ < NKT - 1 else 1
                    for rq in range(RTB):
                        ev = ve(lambda e, rq=rq, tt_=tt_, p=p, nc_=ncols:
                                e.tensor_copy(
                            vrtok[:, rq * L + tt_ * 128:
                                  rq * L + tt_ * 128 + nc_],
                            ptrv[p][:, rq * 128:rq * 128 + nc_]),
                            waits=[ttr] + (list(ast["vr"]) + tok_lora
                                           if tt_ == 0 else []))
                    tok_ptr_free[p] = ev
                    tok_vr_last = ev
                    if tt_ < NKT - 1:
                        tjr = None
                        for rq in range(RTB):
                            tjr = pe(lambda e, rq=rq, vmn=vmn: e.matmul(
                                psB[:, rq * 128:(rq + 1) * 128],
                                vmn[:, rq * 128:(rq + 1) * 128], s_rev[:],
                                start=True, stop=True),
                                waits=[emn, tok_psB4[0]] if rq == 0 else [emn])
                        lo = 3969 - 128 * tt_
                        ncm = 127 if tt_ == 0 else 128
                        evm = None
                        for rq in range(RTB):
                            evm = ve(lambda e, rq=rq, lo=lo, ncm=ncm:
                                     e.tensor_copy(
                                vrtok[:, rq * L + lo:rq * L + lo + ncm],
                                psB[:, rq * 128:rq * 128 + ncm]),
                                waits=[tjr] + (list(ast["vr"]) + tok_lora
                                               if tt_ == 0 else []))
                        tok_psB4 = [evm, evm]
                        tok_t4stg[p] = evm
                    else:
                        tok_t4stg[p] = ev

                if os.environ.get("K_TRUNC"):
                    rec.step("sync", None, waits=rec.barrier())
                    rec.disabled = True
                    tok_prev_block = [None, None, None, None]
                    break
                prev_entry = [tok_tabu[0], tok_tabu[1], tok_psf[0], tok_psf[1],
                              tok_u_last, tok_et_last]
                tok_psB_carry = tok_psB4[0]
                pend_az = [None, None]
                pend_asm = asm_gen(b, tok_vr_last, pend_az)
                pend_b = b

            if pend_asm is not None:
                while pend_az[0] is None:
                    if drain(pend_asm, 1):
                        break
                emit_zturn(pend_b, 0, pend_az[0])
                while not drain(pend_asm, 4):
                    pass
                emit_zturn(pend_b, 1, pend_az[1])
                pend_asm = None
            if tok_zwr:
                tok_prev_block = [ast["zt"], ast["A"],
                                  tok_zwr[-1], tok_ywr[-1]]
            tok_asmC_fin = list(ast["C"])

        with nc.psum_tensor("pax", [64, 1024], F32) as pax:
            tz = [None] * 4
            tzu = [None] * 4
            mm = None
            for q in range(NQ):
                p = q % 4
                tz[p] = dma("sync", "dz%d" % p, zv2[p], d_zscr[q],
                            waits=[tzu[p]] +
                            (tok_zwr + tok_prev_block if q < 4 else []))
                for h in range(2):
                    mm = pe(lambda e, q=q, p=p, h=h: e.matmul(
                        pax[:, h * 512:(h + 1) * 512],
                        a_comb[:, q * 64:(q + 1) * 64],
                        zv2[p][:, h * 512:(h + 1) * 512],
                        start=(q == 0), stop=(q == NQ - 1)),
                        waits=[tz[p], tok_acomb] + tok_prev_block
                        if q == 0 and h == 0 else ([tz[p]] if h == 0 else []))
                tzu[p] = mm
            tok_ax = ve(lambda e: e.tensor_copy(s_ax[:], pax[:]),
                        waits=[mm, tok_prev_block[1]])

        tok_bias = dma("gpsimd", "dbias", biasbf[:], d_bias,
                       waits=[tok_ax] + tok_prev_block)
        tok_bb = tok_bias

        with ExitStack() as pctx:
            pf = [pctx.enter_context(nc.psum_tensor("pf%d" % i, [128, 512], F32))
                  for i in range(8)]
            twu = [None] * 4
            tz2 = [None] * 4
            tz2u = [None] * 4
            tok_fst = []
            tok_pf_free = [tok_bb] * 8
            fsv = [asmB[:, j * 512:(j + 1) * 512] for j in range(4)]
            tok_fsv = [tok_prev_block[1]] * 4
            tok_s2prev = tok_asmC_fin[3] if tok_asmC_fin else None
            NWCH = 8 * NQ
            wstg = [asmA[:, s * 512:(s + 1) * 512] for s in range(8)]
            tw_ld = [None] * NWCH
            tw_cast = [None] * NWCH
            w_init = [tok_ax, tok_bb] + tok_prev_block

            ORDER = []
            for ot_ in range(8):
                ks = range(NQ) if ot_ % 2 == 0 else range(NQ - 1, -1, -1)
                for kq_ in ks:
                    ORDER.append((ot_, kq_))

            def w_load(idx):
                s = idx % 8
                ot_, kq_ = ORDER[idx]
                tw_ld[idx] = dma(
                    "scalar", "dwh%d" % s, wstg[s],
                    d_w[kq_ * 128:(kq_ + 1) * 128,
                        ot_ * 512:(ot_ + 1) * 512],
                    waits=[tw_cast[idx - 8]] if idx >= 8 else w_init)

            PRE = 6
            for i in range(PRE):
                w_load(i)
            cur_z = [None] * 4
            idx = 0
            for ot in range(8):
                ks = range(NQ) if ot % 2 == 0 else range(NQ - 1, -1, -1)
                mm_rt = [None] * 8
                for i, kq in enumerate(ks):
                    p = kq % 4
                    if idx + PRE < NWCH:
                        w_load(idx + PRE)
                    tw_cast[idx] = se(
                        lambda e, p=p, s=idx % 8: e.activation(
                            wv[p], wstg[s], AF.Copy),
                        waits=[tw_ld[idx],
                               twu[p] if twu[p] else tok_prev_block[0]])
                    if cur_z[p] != kq:
                        tz2[p] = dma("sync", "dz%d" % p, zv2[p], d_zscr[kq],
                                     waits=[tz2u[p],
                                            tok_ax if (ot == 0 and i < 4)
                                            else None]
                                     + (tok_zwr if ot == 0 and i < 4 else []))
                        cur_z[p] = kq
                    mm = None
                    for rt in range(8):
                        w0 = []
                        if rt == 0:
                            w0 = [tw_cast[idx], tz2[p]]
                            if i == 0:
                                w0 += [tok_ax, tok_bb]
                        if i == 0:
                            w0 = w0 + [tok_pf_free[rt]]
                        mm = pe(lambda e, rt=rt, p=p, st=(i == 0): e.matmul(
                            pf[rt][:], zv2[p][:, rt * 128:(rt + 1) * 128],
                            wv[p], start=st, stop=False), waits=w0)
                        mm_rt[rt] = mm
                    twu[p] = mm
                    tz2u[p] = mm
                    idx += 1
                for rt in range(8):
                    mm_rt[rt] = pe(lambda e, rt=rt, ot=ot: e.matmul(
                        pf[rt][:], s_ax[:, rt * 128:(rt + 1) * 128],
                        b_comb[:, ot * 512:(ot + 1) * 512],
                        start=False, stop=False),
                        waits=[tok_ax, tok_bcomb] if rt == 0 else [])
                for rt in range(8):
                    mm_rt[rt] = pe(lambda e, rt=rt, ot=ot: e.matmul(
                        pf[rt][:], s_onb[:],
                        biasbf[0:1, ot * 512:(ot + 1) * 512],
                        start=False, stop=True),
                        waits=[tok_bias, tok_misc] if rt == 0 else [])
                for rt in range(8):
                    j = (ot * 8 + rt) % 4
                    fb = fsv[j]
                    e1 = ve(lambda e, rt=rt, fb=fb: e.tensor_copy(
                        fb, pf[rt][:]),
                        waits=[mm_rt[rt], tok_fsv[j]])
                    tok_pf_free[rt] = e1
                    s1 = ve(lambda e, rt=rt, ot=ot, fb=fb: e.reduce_sum(
                        fsum[:, rt * 8 + ot:rt * 8 + ot + 1], fb,
                        axis=AX.X), waits=[e1])
                    s2 = se(lambda e, rt=rt, ot=ot, fb=fb: e.activation(
                        tr512, fb, AF.Square,
                        accum_out=fsq[:, rt * 8 + ot:rt * 8 + ot + 1]),
                        waits=[e1, s1, tok_s2prev])
                    tok_s2prev = s2
                    tok_fw = dma("sync", "dfw%d" % j,
                                 d_fscr[rt][:, ot * 512:(ot + 1) * 512],
                                 fb, waits=[s1, s2])
                    tok_fst.append((s1, s2, tok_fw))
                    tok_fsv[j] = tok_fw

            tok_fstats = [t for tpl in tok_fst for t in tpl[:2]]
            tok_fwr = [tpl[2] for tpl in tok_fst]

        LH2 = L // 2
        tok_out = []
        tok_tf = [None, None]
        tok_ty = [None, None]
        tok_p1 = [None, None]
        tok_p2 = [None, None]
        tok_do = [None, None]
        tst4s = []
        for rt in range(8):
            b8 = rt * 8
            c1 = ve(lambda e, rt=rt, o=stat4[:, b8:b8 + 1]: e.reduce_sum(
                o, fsum[:, rt * 8:(rt + 1) * 8], axis=AX.X),
                waits=tok_fstats if rt == 0 else [])
            c2 = ve(lambda e, rt=rt, o=stat4[:, b8 + 1:b8 + 2]: e.reduce_sum(
                o, fsq[:, rt * 8:(rt + 1) * 8], axis=AX.X), waits=[c1])
            tst4s.append(ln_stats(stat4, b8, c1, c2))
        for rt in range(8):
            b8 = rt * 8
            for h in range(2):
                fh = asmA[:, h * LH2:(h + 1) * LH2]
                yh = asmB[:, h * LH2:(h + 1) * LH2]
                oh = asmC[:, h * LH2:(h + 1) * LH2]
                tf = dma("sync", "dfr", fh,
                         d_fscr[rt][:, h * LH2:(h + 1) * LH2],
                         waits=[tok_do[h], tw_cast[NWCH - 1] if rt == 0 else None]
                         + (tok_fwr if rt == 0 else []))
                ty = dma("sync", "dyr", yh,
                         d_yscr[rt][:, h * LH2:(h + 1) * LH2],
                         waits=[tok_p2[h]]
                         + (tok_ywr + tok_fwr if rt == 0 else []))
                p1 = ve(lambda e, oh=oh, fh=fh, m4=stat4[:, b8 + 2:b8 + 3],
                        r4=stat4[:, b8 + 5:b8 + 6]: e.tensor_scalar(
                    oh, fh, m4, r4, op0=ALU.subtract, op1=ALU.mult),
                    waits=[tf, tst4s[rt], tok_p2[h]]
                    + (tok_asmC_fin if rt == 0 else []))
                p2 = ve(lambda e, fh=fh, oh=oh, yh=yh: e.tensor_tensor(
                    fh, oh, yh, op=ALU.add), waits=[p1, ty])
                tok_do[h] = dma("sync", "dout%d" % h,
                                d_out[rt * 128:(rt + 1) * 128,
                                      h * LH2:(h + 1) * LH2],
                                fh, waits=[p2])
                tok_out.append(tok_do[h])
                tok_tf[h] = tf
                tok_ty[h] = ty
                tok_p1[h] = p1
                tok_p2[h] = p2
        rec.step("sync", None, waits=tok_out)

        with nc.Block() as block:
            def mk(eng_name):
                def emit(eng):
                    for (e_, fn, waits, inc, n) in rec.steps:
                        if e_ != eng_name:
                            continue
                        for s, v in waits:
                            eng.wait_ge(sems[s], v)
                        if fn is None:
                            continue
                        inst = fn(eng)
                        if inc is not None:
                            inst.then_inc(sems[inc], n)
                return emit

            block.sync(mk("sync"))
            block.gpsimd(mk("gpsimd"))
            block.tensor(mk("tensor"))
            block.vector(mk("vector"))
            block.scalar(mk("scalar"))

    globals()["_LAST_REC"] = rec
    return nc


_NC_CACHE = None


def kernel(L0_attn_x, L0_ffn_gate, L0_ffn_kernel, L0_ffn_bias, L0_ffn_lora_A,
           L0_ffn_lora_B, L0_ffn_lora_A_shared, L0_ffn_lora_B_shared):
    global _NC_CACHE
    x = np.ascontiguousarray(np.asarray(L0_attn_x, dtype=np.float32).reshape(8192, L))
    gate = np.asarray(L0_ffn_gate, dtype=np.float32)
    w = np.ascontiguousarray(np.asarray(L0_ffn_kernel, dtype=np.float32))
    bias = np.asarray(L0_ffn_bias, dtype=np.float32).reshape(1, L)
    lora_A = np.asarray(L0_ffn_lora_A, dtype=np.float32)
    lora_B = np.asarray(L0_ffn_lora_B, dtype=np.float32)
    A_sh = np.asarray(L0_ffn_lora_A_shared, dtype=np.float32)
    B_sh = np.asarray(L0_ffn_lora_B_shared, dtype=np.float32)

    tb = _make_tables()

    lA = np.ascontiguousarray(
        lora_A.transpose(0, 2, 1).reshape(16, NQ, 128, 32)
        .transpose(0, 2, 1, 3).reshape(16, 128, 1024))
    lAs = np.ascontiguousarray(
        A_sh.transpose(1, 0).reshape(NQ, 128, 32).transpose(1, 0, 2)
        .reshape(128, 1024))
    lB = np.ascontiguousarray(lora_B.transpose(0, 2, 1))
    lBs = np.ascontiguousarray(B_sh.transpose(1, 0))

    if _NC_CACHE is None:
        _NC_CACHE = build_program()
    nc = _NC_CACHE

    in_maps = []
    for c in range(NCORE):
        shard = np.ascontiguousarray(x[c * R:(c + 1) * R])
        shard_T = np.ascontiguousarray(shard.T)
        shard_Tr = np.ascontiguousarray(np.roll(shard_T[::-1, :], 1, axis=0))
        gb = np.ascontiguousarray(np.tile(gate[c:c + 1, :], (128, 1)))
        in_maps.append({
            "x": shard, "xT": shard_T, "xTr": shard_Tr, "w": w, "biasr": bias, "gateb": gb,
            "lA": lA, "lAs": lAs, "lB": lB, "lBs": lBs,
            "t1c": tb["t1c"], "t1s": tb["t1s"], "t2c": tb["t2c"],
            "t3c": tb["t3c"], "t4c": tb["t4c"], "t4s": tb["t4s"],
            "identb": tb["identb"], "revb": tb["revb"],
            "identf": tb["identf"],
            "onesb": tb["onesb"], "onesf": tb["onesf"],
        })

    res = run_bass_kernel_spmd(nc, in_maps, core_ids=list(range(NCORE)))
    out = np.concatenate([np.asarray(r["out"]) for r in res.results], axis=0)
    return out.reshape(2, 4096, L).astype(np.float32)


# version 19
# speedup vs baseline: 1.0158x; 1.0158x over previous
import math
import os
from contextlib import ExitStack

import numpy as np
import ml_dtypes

import concourse.bass as bass
import concourse.mybir as mybir
from concourse.bass_utils import run_bass_kernel_spmd

F32 = mybir.dt.float32
BF16 = mybir.dt.bfloat16
AF = mybir.ActivationFunctionType
ALU = mybir.AluOpType
AX = mybir.AxisListType

NCORE = 8
R = 1024
L = 4096
NQ = 32
NKT = 17
KH = 2049
KHP = NKT * 128
RB = 256
NRB = R // RB
RTB = RB // 128
NRT = R // 128
EPS = 1e-5

_BF = ml_dtypes.bfloat16
_TABLES = None


def _make_tables():
    global _TABLES
    if _TABLES is not None:
        return _TABLES
    t = np.arange(L, dtype=np.float64)
    k = np.arange(KHP, dtype=np.float64)
    ang = 2.0 * np.pi / L
    kmask = (k <= 2048).astype(np.float64)
    wk = np.where((k == 0) | (k == 2048), 1.0, 2.0) * kmask

    th = np.arange(KHP, dtype=np.float64)
    wt = np.where((th == 0) | (th == 2048), 0.5, np.where(th > 2048, 0.0, 1.0))
    phase = np.outer(th, k) * ang
    t1c = np.cos(phase) * kmask[None, :] * wt[:, None]
    t1s = np.sin(phase) * kmask[None, :] * wt[:, None]

    def fwd_layout(m):
        return np.ascontiguousarray(
            m.reshape(NKT, 128, NKT, 128).transpose(2, 1, 0, 3)).astype(_BF)

    tau = np.arange(KHP, dtype=np.float64)
    taumask = (tau <= 2048).astype(np.float64)
    t2 = np.cos(np.outer(k, tau) * ang) * (wk / L)[:, None] * taumask[None, :]

    def half_layout(m):
        return np.ascontiguousarray(
            m.reshape(NKT, 128, NKT, 128).transpose(2, 1, 0, 3)).astype(_BF)

    d = np.arange(KHP, dtype=np.float64)
    wd = np.where((d == 0) | (d >= 2049), 0.0, np.where(d == 2048, 1.0, 2.0))
    t3 = np.cos(np.outer(d, k) * ang) * wd[:, None] * kmask[None, :]

    t4base = np.outer(k, th) * ang
    t4c = np.cos(t4base) * (wk / L)[:, None]
    t4s = np.sin(t4base) * (wk / L)[:, None]

    def inv_layout(m):
        return np.ascontiguousarray(
            m.reshape(NKT, 128, NKT, 128).transpose(2, 1, 0, 3)).astype(_BF)

    _out = dict(
        t1c=fwd_layout(t1c), t1s=fwd_layout(t1s),
        t2c=half_layout(t2), t3c=half_layout(t3),
        t4c=inv_layout(t4c), t4s=inv_layout(t4s),
        identb=np.asarray(np.eye(128, dtype=np.float32).astype(_BF)),
        revb=np.asarray(np.fliplr(np.eye(128)).astype(np.float32).astype(_BF)),
        identf=np.eye(128, dtype=np.float32),
        onesb=np.asarray(np.ones((1, 128), np.float32).astype(_BF)),
        onesf=np.ones((1, 128), np.float32),
    )
    globals()["_TABLES"] = _out
    return _out


class Rec:

    def __init__(self):
        self.steps = []
        self.cnt = {}
        self.disabled = False

    def now(self, sem):
        return (sem, self.cnt.get(sem, 0))

    def barrier(self):
        return [self.now(s) for s in self.cnt]

    def step(self, eng, fn, waits=(), inc=None, n=1):
        w = {}
        for tk in waits:
            if tk is None:
                continue
            s, v = tk
            if v > 0:
                w[s] = max(w.get(s, 0), v)
        tok = None
        if inc is not None:
            self.cnt[inc] = self.cnt.get(inc, 0) + n
            tok = (inc, self.cnt[inc])
        if not self.disabled:
            self.steps.append((eng, fn, sorted(w.items()), inc, n))
        return tok


def build_program():
    nc = bass.Bass("TRN2", target_bir_lowering=False)
    tb = _make_tables()

    d_x = nc.dram_tensor("x", [R, L], F32, kind="ExternalInput").ap()
    d_xT = nc.dram_tensor("xT", [L, R], F32, kind="ExternalInput").ap()
    d_xTr = nc.dram_tensor("xTr", [L, R], F32, kind="ExternalInput").ap()
    d_w = nc.dram_tensor("w", [L, L], F32, kind="ExternalInput").ap()
    d_bias = nc.dram_tensor("biasr", [1, L], F32, kind="ExternalInput").ap()
    d_gate = nc.dram_tensor("gateb", [128, 16], F32, kind="ExternalInput").ap()
    d_lA = nc.dram_tensor("lA", [16, 128, 1024], F32, kind="ExternalInput").ap()
    d_lAs = nc.dram_tensor("lAs", [128, 1024], F32, kind="ExternalInput").ap()
    d_lB = nc.dram_tensor("lB", [16, 32, L], F32, kind="ExternalInput").ap()
    d_lBs = nc.dram_tensor("lBs", [32, L], F32, kind="ExternalInput").ap()
    d_t1c = nc.dram_tensor("t1c", list(tb["t1c"].shape), BF16, kind="ExternalInput").ap()
    d_t1s = nc.dram_tensor("t1s", list(tb["t1s"].shape), BF16, kind="ExternalInput").ap()
    d_t2c = nc.dram_tensor("t2c", list(tb["t2c"].shape), BF16, kind="ExternalInput").ap()
    d_t3c = nc.dram_tensor("t3c", list(tb["t3c"].shape), BF16, kind="ExternalInput").ap()
    d_t4c = nc.dram_tensor("t4c", list(tb["t4c"].shape), BF16, kind="ExternalInput").ap()
    d_t4s = nc.dram_tensor("t4s", list(tb["t4s"].shape), BF16, kind="ExternalInput").ap()
    d_idb = nc.dram_tensor("identb", [128, 128], BF16, kind="ExternalInput").ap()
    d_rev = nc.dram_tensor("revb", [128, 128], BF16, kind="ExternalInput").ap()
    d_idf = nc.dram_tensor("identf", [128, 128], F32, kind="ExternalInput").ap()
    d_onb = nc.dram_tensor("onesb", [1, 128], BF16, kind="ExternalInput").ap()
    d_onf = nc.dram_tensor("onesf", [1, 128], F32, kind="ExternalInput").ap()

    d_out = nc.dram_tensor("out", [R, L], F32, kind="ExternalOutput").ap()
    d_yscr = nc.dram_tensor("y_scr", [NRT, 128, L], F32).ap()
    d_zscr = nc.dram_tensor("z_scr", [NQ, 128, R], BF16).ap()
    d_fscr = nc.dram_tensor("f_scr", [NRT, 128, L], F32).ap()

    rec = Rec()

    def dma(eng, sem, out, in_, waits=()):
        return rec.step(eng, lambda e, o=out, i=in_: e.dma_start(out=o, in_=i),
                        waits=waits, inc=sem, n=16)

    def tab2(sem, dst, src, waits=()):
        t1 = dma("sync", sem, dst[:, 0:KHP // 2], src[:, 0:KHP // 2],
                 waits=waits)
        t2 = dma("sync", sem + "b", dst[:, KHP // 2:KHP], src[:, KHP // 2:KHP],
                 waits=waits)
        return [t1, t2]

    def pe(fn, waits=()):
        return rec.step("tensor", fn, waits=waits, inc="pe")

    def ve(fn, waits=()):
        return rec.step("vector", fn, waits=waits, inc="ve")

    def se(fn, waits=()):
        return rec.step("scalar", fn, waits=waits, inc="se")

    with ExitStack() as ctx:
        ee = ctx.enter_context

        s_idb = ee(nc.sbuf_tensor("s_idb", [128, 128], BF16))
        s_rev = ee(nc.sbuf_tensor("s_rev", [128, 128], BF16))
        s_idf = ee(nc.sbuf_tensor("s_idf", [128, 128], F32))
        s_onb = ee(nc.sbuf_tensor("s_onb", [1, 128], BF16))
        s_onf = ee(nc.sbuf_tensor("s_onf", [1, 128], F32))
        s_gate = ee(nc.sbuf_tensor("s_gate", [128, 16], F32))
        stat1 = ee(nc.sbuf_tensor("stat1", [128, 64], F32))
        stat2 = ee(nc.sbuf_tensor("stat2", [128, 64], F32))
        stat3 = ee(nc.sbuf_tensor("stat3", [128, 64], F32))
        stat4 = ee(nc.sbuf_tensor("stat4", [128, 64], F32))
        fsum = ee(nc.sbuf_tensor("fsum", [128, 64], F32))
        fsq = ee(nc.sbuf_tensor("fsq", [128, 64], F32))
        mcol = ee(nc.sbuf_tensor("mcol", [128, 16], F32))
        m8ab = ee(nc.sbuf_tensor("m8ab", [128, 16], F32))
        m8all = ee(nc.sbuf_tensor("m8all", [128, 32], F32))
        rstdrow = ee(nc.sbuf_tensor("rstdrow", [1, R], F32))
        a_comb = ee(nc.sbuf_tensor("a_comb", [128, 2048], BF16))
        b_comb = ee(nc.sbuf_tensor("b_comb", [64, L], BF16))
        s_ax = ee(nc.sbuf_tensor("s_ax", [64, 1024], BF16))
        biasbf = ee(nc.sbuf_tensor("biasbf", [1, L], BF16))

        ar45 = ee(nc.sbuf_tensor("ar45", [128, 4 * NKT * RB], BF16))
        u_r = ee(nc.sbuf_tensor("u_r", [128, NKT * RB], BF16))
        u_i = ee(nc.sbuf_tensor("u_i", [128, NKT * RB], BF16))
        slabP = ee(nc.sbuf_tensor("slabP", [128, NKT * RB], BF16))
        ctok = ee(nc.sbuf_tensor("ctok", [128, RTB * KHP], BF16))
        etok = ee(nc.sbuf_tensor("etok", [128, RTB * KHP], BF16))
        tabA = ee(nc.sbuf_tensor("tabA", [128, 2 * KHP], BF16))
        tabB = ee(nc.sbuf_tensor("tabB", [128, 2 * KHP], BF16))
        stgf = ee(nc.sbuf_tensor("stgf", [128, 512], F32))
        bcst = ee(nc.sbuf_tensor("bcst", [128, RB], F32))
        stgc = ee(nc.sbuf_tensor("stgc", [128, 512], BF16))
        asmA = ee(nc.sbuf_tensor("asmA", [128, L], F32))
        asmB = ee(nc.sbuf_tensor("asmB", [128, L], F32))
        asmC = ee(nc.sbuf_tensor("asmC", [128, L], F32))

        LH = L // 2
        lb_guard = nc.sbuf_tensor("lb_st", [32, 2 * LH], F32)
        lb_st = lb_guard.__enter__()
        be_guard = nc.sbuf_tensor("beff", [32, 2 * LH], F32)
        beff = be_guard.__enter__()
        lb_v = [lb_st[:, 0:LH], lb_st[:, LH:2 * LH]]
        be_v = [beff[:, 0:LH], beff[:, LH:2 * LH]]
        be_guard.__exit__(None, None, None)
        lb_guard.__exit__(None, None, None)

        vrtok = ee(nc.sbuf_tensor("vrtok", [128, RTB * L], BF16))
        ztok = ee(nc.sbuf_tensor("ztok", [128, L], BF16))
        zstg = ee(nc.sbuf_tensor("zstg", [128, L], BF16))
        trash = vrtok[:, 0:L]

        HC = NKT * RB
        xt_v = ar45[:, 0:HC]
        xtr_v = ar45[:, HC:2 * HC]
        ue_v = ar45[:, 2 * HC:3 * HC]
        uo_v = ar45[:, 3 * HC:4 * HC]
        v_r = ar45[:, 2 * HC:3 * HC]
        v_i = ar45[:, 3 * HC:4 * HC]
        tabAv = [tabA[:, 0:KHP], tabA[:, KHP:2 * KHP]]
        tabB1 = tabB[:, 0:KHP]
        tabB4 = [tabB[:, 0:KHP], tabB[:, KHP:2 * KHP]]
        stgm = tabB[:, 0:2 * KHP]
        sq1 = stgf[:, 0:RB]
        sq2 = stgf[:, RB:2 * RB]
        la_v = [asmC[:, 0:1024], asmC[:, 1024:2048]]
        ae_v = [asmC[:, 2048:3072], asmC[:, 3072:4096]]
        xv = [asmA[:], asmB[:]]
        wv = [ztok[:, i * 512:(i + 1) * 512] for i in range(4)]
        zv2 = [zstg[:, i * 1024:(i + 1) * 1024] for i in range(4)]
        fstg = stgf[:, 0:512]
        tr512 = trash[:, 0:512]

        sem_names = ["pe", "ve", "se", "dtab0", "dtab1", "dtabS0", "dtabS1",
                     "dtab0b", "dtab1b", "dtabS0b", "dtabS1b",
                     "dx0", "dx1", "dxt",
                     "dwh0", "dwh1", "dwh2", "dwh3",
                     "dwh4", "dwh5", "dwh6", "dwh7",
                     "dz0", "dz1", "dz2", "dz3", "dzw",
                     "dyw", "dyr", "dfw0", "dfw1", "dfw2", "dfw3", "dfr",
                     "dout0", "dout1", "dmisc",
                     "dlA0", "dlA1", "dlB0", "dlB1", "dbias"]
        sems = {s: ee(nc.semaphore(s)) for s in sem_names}

        dma("sync", "dmisc", s_idb[:], d_idb)
        dma("sync", "dmisc", s_rev[:], d_rev)
        dma("sync", "dmisc", s_idf[:], d_idf)
        dma("sync", "dmisc", s_onb[:], d_onb)
        dma("sync", "dmisc", s_onf[:], d_onf)
        tok_misc = dma("sync", "dmisc", s_gate[:], d_gate)

        lt = {}

        def lora_gen():
            tok_bc = None
            tBs_t = None
            for h in range(2):
                hs = slice(h * LH, (h + 1) * LH)
                tB = [None, None]
                last_b = None
                pb_use = [tok_bc, tok_bc]
                for n in range(16):
                    p = n % 2
                    tB[p] = dma("sync", "dlB%d" % p, lb_v[p], d_lB[n][:, hs],
                                waits=[pb_use[p]])
                    gB = s_gate[0:32, n:n + 1]
                    dst_b = be_v[(n + 1) % 2]
                    src_b = be_v[n % 2]
                    if n == 0:
                        last_b = ve(lambda e, d0=dst_b, s0=lb_v[p], g=gB:
                                    e.tensor_scalar(d0, s0, g, None,
                                                    op0=ALU.mult),
                                    waits=[tB[p], tok_misc, tok_bc])
                    else:
                        last_b = ve(lambda e, d0=dst_b, s0=lb_v[p], g=gB,
                                    acc=src_b:
                                    e.scalar_tensor_tensor(d0, s0, g, acc,
                                                           op0=ALU.mult,
                                                           op1=ALU.add),
                                    waits=[tB[p], last_b])
                    pb_use[p] = last_b
                    yield
                be_fin = be_v[0]
                t_bc0 = ve(lambda e, hs=hs, be_fin=be_fin: e.tensor_copy(
                    b_comb[0:32, hs], be_fin), waits=[last_b])
                tBs_t = dma("sync", "dlB1", lb_v[1], d_lBs[:, hs],
                            waits=[last_b, tBs_t])
                tok_bc = ve(lambda e, hs=hs: e.tensor_copy(
                    b_comb[32:64, hs], lb_v[1]), waits=[tBs_t, t_bc0])
                yield
            lt["bcomb"] = tok_bc
            tA = [None, None]
            last_a = None
            pa_use = [None, None]
            la_ld = [asmC[:, 0:1024], asmC[:, 1024:2048]]
            for n in range(16):
                p = n % 2
                tA[p] = dma("sync", "dlA%d" % p, la_ld[p], d_lA[n],
                            waits=[pa_use[p]])
                gA = s_gate[:, n:n + 1]
                dst_a = ae_v[(n + 1) % 2]
                src_a = ae_v[n % 2]
                if n == 0:
                    last_a = ve(lambda e, d0=dst_a, s0=la_ld[p], g=gA:
                                e.tensor_scalar(d0, s0, g, None, op0=ALU.mult),
                                waits=[tA[p], tok_misc])
                else:
                    last_a = ve(lambda e, d0=dst_a, s0=la_ld[p], g=gA, acc=src_a:
                                e.scalar_tensor_tensor(d0, s0, g, acc,
                                                       op0=ALU.mult,
                                                       op1=ALU.add),
                                waits=[tA[p], last_a])
                pa_use[p] = last_a
                yield
            ae_fin = ae_v[0]
            ac3 = a_comb.ap().rearrange("p (q r) -> p q r", r=64)
            ae3 = ae_fin.rearrange("p (q r) -> p q r", r=32)
            t_ac0 = ve(lambda e: e.tensor_copy(ac3[:, :, 0:32], ae3),
                       waits=[last_a])
            tAs = dma("sync", "dlA1", la_ld[1], d_lAs, waits=[last_a])
            las3 = la_ld[1].rearrange("p (q r) -> p q r", r=32)
            lt["acomb"] = ve(lambda e: e.tensor_copy(ac3[:, :, 32:64], las3),
                             waits=[tAs, t_ac0])
            yield

        def drain(gen, k):
            if gen is None:
                return True
            for _ in range(k):
                try:
                    next(gen)
                except StopIteration:
                    return True
            return False

        def ln_stats(stat, base, tsum, tsq):
            c = lambda j: stat[:, base + j:base + j + 1]
            t1 = ve(lambda e: e.tensor_scalar(c(2), c(0), 1.0 / L, None, op0=ALU.mult),
                    waits=[tsum])
            t2 = ve(lambda e: e.tensor_scalar(c(3), c(1), 1.0 / L, None, op0=ALU.mult),
                    waits=[tsq])
            t3 = ve(lambda e: e.tensor_tensor(c(4), c(2), c(2), op=ALU.mult), waits=[t1])
            t4 = ve(lambda e: e.tensor_tensor(c(4), c(3), c(4), op=ALU.subtract),
                    waits=[t2, t3])
            t5 = ve(lambda e: e.tensor_scalar(c(4), c(4), EPS, None, op0=ALU.add),
                    waits=[t4])
            t6 = ve(lambda e: e.reciprocal(c(3), c(4)), waits=[t5])
            t7 = se(lambda e: e.activation(c(5), c(3), AF.Sqrt), waits=[t6])
            t8 = ve(lambda e: e.tensor_scalar(c(6), c(2), -1.0, c(5),
                                              op0=ALU.mult, op1=ALU.mult), waits=[t7])
            return t8

        with nc.psum_tensor("pmA", [128, RB], F32) as pmA, \
             nc.psum_tensor("pmB", [128, RB], F32) as pmB, \
             nc.psum_tensor("pmC", [128, RB], F32) as pmC, \
             nc.psum_tensor("pmD", [128, RB], F32) as pmD, \
             nc.psum_tensor("ptrA", [128, 512], BF16) as ptrA, \
             nc.psum_tensor("ptrB", [128, 512], BF16) as ptrB, \
             nc.psum_tensor("psB", [128, 512], F32) as psB:

            pr = pmA[:]
            pi = pmB[:]
            prv = [pmA[:], pmC[:]]
            piv = [pmB[:], pmD[:]]
            ptrv = [ptrA[:], ptrB[:]]

            tok_stats = [None] * NRT
            tok_rrow = [None] * NRT
            tld = [None, None]
            tuse = [None, None]
            tok_psB = None
            for rt in range(NRT):
                p = rt % 2
                tld[p] = dma("sync", "dx%d" % p, xv[p],
                             d_x[rt * 128:(rt + 1) * 128, :],
                             waits=[tuse[p]] if tuse[p] else [])
                b8 = rt * 8
                ts1 = ve(lambda e, s0=xv[p], o=stat1[:, b8:b8 + 1]:
                         e.reduce_sum(o, s0, axis=AX.X), waits=[tld[p]])
                ts2 = se(lambda e, s0=xv[p], o=stat1[:, b8 + 1:b8 + 2]:
                         e.activation(etok[:, 0:L], s0, AF.Square, accum_out=o),
                         waits=[tld[p], ts1])
                tuse[p] = ts2
                tok_stats[rt] = ln_stats(stat1, b8, ts1, ts2)
                tt = pe(lambda e, c=stat1[:, b8 + 5:b8 + 6]:
                        e.transpose(psB[0:1, 0:128], c, s_idf[:]),
                        waits=[tok_stats[rt], tok_misc, tok_psB])
                tok_rrow[rt] = ve(lambda e, rt=rt: e.tensor_copy(
                    rstdrow[0:1, rt * 128:(rt + 1) * 128], psB[0:1, 0:128]),
                    waits=[tt])
                tok_psB = tok_rrow[rt]
            tok_allstats = [t for t in tok_stats + tok_rrow]

            tok_prev_block = None
            tok_zwr = []
            tok_ywr = []
            tok_ptr_free = [None, None]

            ast = {"A": None, "B": None, "C": [], "zt": None, "vr": []}

            def asm_gen(b, tok_vr_done, azq):
                for rq in range(RTB):
                    rt = b * RTB + rq
                    p = rt % 2
                    b8 = rt * 8
                    m1 = stat1[:, b8 + 2:b8 + 3]
                    r1 = stat1[:, b8 + 5:b8 + 6]
                    tx = dma("sync", "dx%d" % p, asmA[:],
                             d_x[rt * 128:(rt + 1) * 128, :],
                             waits=[ast["A"]] if ast["A"] else tok_allstats)
                    yield
                    a1 = ve(lambda e, m1=m1, r1=r1: e.tensor_scalar(
                        asmB[:], asmA[:], m1, r1,
                        op0=ALU.subtract, op1=ALU.mult),
                        waits=[tx] + ([ast["B"]] if ast["B"] else tok_allstats))
                    yield
                    a2 = ve(lambda e, rq=rq: e.tensor_tensor(
                        asmC[:], asmB[:], vrtok[:, rq * L:(rq + 1) * L],
                        op=ALU.add),
                        waits=[a1, tok_vr_done] + list(ast["C"]))
                    yield
                    s1 = ve(lambda e, o=stat2[:, b8:b8 + 1]:
                            e.reduce_sum(o, asmC[:], axis=AX.X), waits=[a2])
                    yield
                    s2 = se(lambda e, o=stat2[:, b8 + 1:b8 + 2]:
                            e.activation(trash[:], asmC[:], AF.Square,
                                         accum_out=o), waits=[a2, s1])
                    yield
                    tst = ln_stats(stat2, b8, s1, s2)
                    yield
                    a3 = ve(lambda e, m2=stat2[:, b8 + 2:b8 + 3],
                            r2=stat2[:, b8 + 5:b8 + 6]: e.tensor_scalar(
                        asmB[:], asmC[:], m2, r2,
                        op0=ALU.subtract, op1=ALU.mult), waits=[tst, s2, a2])
                    yield
                    a4 = ve(lambda e: e.tensor_tensor(
                        asmC[:], asmB[:], asmA[:], op=ALU.add), waits=[a3])
                    yield
                    tok_ywr.append(dma("sync", "dyw", d_yscr[rt], asmC[:],
                                       waits=[a4]))
                    yield
                    s3 = ve(lambda e, o=stat3[:, b8:b8 + 1]:
                            e.reduce_sum(o, asmC[:], axis=AX.X), waits=[a4])
                    yield
                    s4 = se(lambda e, o=stat3[:, b8 + 1:b8 + 2]:
                            e.activation(trash[:], asmC[:], AF.Square,
                                         accum_out=o), waits=[a4, s3])
                    yield
                    tst3 = ln_stats(stat3, b8, s3, s4)
                    yield
                    az = se(lambda e, nm=stat3[:, b8 + 6:b8 + 7],
                            r3=stat3[:, b8 + 5:b8 + 6]: e.activation(
                        ztok[:], asmC[:], AF.Identity, bias=nm, scale=r3),
                        waits=[tst3, s4] + ([ast["zt"]] if ast["zt"]
                                            else list(tok_lora)))
                    azq[rq] = az
                    ast["A"] = a4
                    ast["B"] = a4
                    ast["C"] = [tok_ywr[-1], s3, s4, az]
                    ast["vr"] = [a2, s4]
                    yield

            def emit_zturn(bz, rq, az):
                rt = bz * RTB + rq
                ev = None
                ttr = None
                for qg in range(8):
                    pz = qg % 2
                    for j in range(4):
                        q = qg * 4 + j
                        ttr = pe(lambda e, q=q, j=j, pz=pz: e.transpose(
                            ptrv[pz][:, j * 128:(j + 1) * 128],
                            ztok[:, q * 128:(q + 1) * 128], s_idb[:]),
                            waits=[az, tok_ptr_free[pz]] if j == 0 else [az])
                    ev = ve(lambda e, qg=qg, pz=pz: e.tensor_copy(
                        zstg[:, qg * 512:(qg + 1) * 512],
                        ptrv[pz][:, 0:512]),
                        waits=[ttr, tok_zwr[-1] if (qg == 0 and tok_zwr)
                               else None]
                        + (tok_lora if qg == 0 and not tok_zwr else []))
                    tok_ptr_free[pz] = ev
                zdst = d_zscr[:, :, rt * 128:(rt + 1) * 128].transpose([1, 0, 2])
                zsrc = zstg.ap().rearrange("p (q r) -> p q r", r=128)
                tok_zwr.append(dma("sync", "dzw", zdst, zsrc, waits=[ev]))
                ast["zt"] = ttr

            prev_entry = None
            prev_fold = []
            lgen = lora_gen()
            pend_asm = None
            pend_az = None
            pend_b = None
            tok_psB_carry = tok_psB
            tok_lora = []
            tok_acomb = None
            tok_bcomb = None
            for b in range(NRB):
                rows = slice(b * RB, (b + 1) * RB)
                entry = [] if b == 0 else list(prev_entry)

                t_xt0 = dma("gpsimd", "dxt",
                            xt_v.rearrange("p (q r) -> p q r", r=RB),
                            d_xT[0:KHP, rows].rearrange("(q p) r -> p q r",
                                                        p=128),
                            waits=prev_fold)
                t_xt = dma("gpsimd", "dxt",
                           xtr_v.rearrange("p (q r) -> p q r", r=RB),
                           d_xTr[0:KHP, rows].rearrange("(q p) r -> p q r",
                                                        p=128),
                           waits=[])
                t_ue = ve(lambda e: e.tensor_tensor(ue_v, xt_v, xtr_v, op=ALU.add),
                          waits=[t_xt] + entry)
                t_uo = ve(lambda e: e.tensor_tensor(uo_v, xt_v, xtr_v,
                                                    op=ALU.subtract),
                          waits=[t_xt, t_ue])
                prev_fold = [t_ue, t_uo]
                t_k1 = pe(lambda e, rows=rows: e.matmul(
                    psB[:, 0:RB], s_onf[:], rstdrow[0:1, rows],
                    start=True, stop=True),
                    waits=[tok_rrow[2 * b], tok_rrow[2 * b + 1], tok_misc,
                           tok_psB_carry])
                t_rb = ve(lambda e: e.tensor_copy(bcst[:], psB[:, 0:RB]),
                          waits=[t_k1])

                tok_tab = [None, None]
                tok_tabu = [None, None]
                tok_sinuse = None
                tok_psf = [None, None]
                tok_u_last = None
                tok_P_last = None
                for kt in range(NKT):
                    p = kt % 2
                    tok_tab[p] = tab2("dtab%d" % p, tabAv[p],
                                      d_t1c[kt].rearrange("p q k -> p (q k)"),
                                      waits=[tok_tabu[p]] if tok_tabu[p]
                                      else entry)
                    tokb = tab2("dtabS0", tabB1,
                                d_t1s[kt].rearrange("p q k -> p (q k)"),
                                waits=[tok_sinuse] if tok_sinuse else entry)
                    mm = None
                    for q in range(NKT):
                        mm = pe(lambda e, q=q, p=p: e.matmul(
                            prv[p], tabAv[p][:, q * 128:(q + 1) * 128],
                            ue_v[:, q * RB:(q + 1) * RB],
                            start=(q == 0), stop=(q == NKT - 1)),
                            waits=tok_tab[p] + [t_ue, tok_psf[p]]
                            if q == 0 else [])
                    sl = slice(kt * RB, (kt + 1) * RB)
                    e1 = ve(lambda e, sl=sl, p=p: e.tensor_tensor(
                        u_r[:, sl], prv[p], bcst[:], op=ALU.mult), waits=[mm, t_rb])
                    for q in range(NKT):
                        mm = pe(lambda e, q=q, p=p: e.matmul(
                            piv[p], tabB1[:, q * 128:(q + 1) * 128],
                            uo_v[:, q * RB:(q + 1) * RB],
                            start=(q == 0), stop=(q == NKT - 1)),
                            waits=tokb + [t_uo, tok_psf[p]] if q == 0 else [])
                    tok_tabu[p] = mm
                    tok_sinuse = mm
                    e2 = ve(lambda e, sl=sl, p=p: e.tensor_tensor(
                        u_i[:, sl], piv[p], bcst[:], op=ALU.mult), waits=[mm, e1])
                    q1 = se(lambda e, sl=sl: e.activation(
                        sq1, u_r[:, sl], AF.Square), waits=[e1, tok_P_last])
                    q2 = se(lambda e, sl=sl: e.activation(
                        sq2, u_i[:, sl], AF.Square), waits=[e2, tok_P_last])
                    e3 = ve(lambda e, sl=sl: e.tensor_tensor(
                        slabP[:, sl], sq1, sq2, op=ALU.add), waits=[q1, q2])
                    if kt == 0:
                        e3 = ve(lambda e: e.memzero(slabP[0:1, 0:RB]), waits=[e3])
                    tok_psf[p] = e2
                    tok_u_last = e2
                    tok_P_last = e3
                    if b == 0:
                        drain(lgen, 1)
                    else:
                        drain(pend_asm, 1)

                if pend_asm is not None:
                    while pend_az[0] is None:
                        if drain(pend_asm, 1):
                            break
                    emit_zturn(pend_b, 0, pend_az[0])

                tok_tab = [None, None]
                tok_stgc_free = [None, None]
                tok_ct_last = None
                stgcv = [stgc[:, 0:RB], stgc[:, RB:2 * RB]]
                for tt_ in range(NKT):
                    p = tt_ % 2
                    tok_tab[p] = tab2("dtab%d" % p, tabAv[p][:, 0:KHP],
                                      d_t2c[tt_].rearrange("p q k -> p (q k)"),
                                      waits=[tok_tabu[p]])
                    mm = None
                    for kq in range(NKT):
                        mm = pe(lambda e, kq=kq, p=p: e.matmul(
                            prv[p], tabAv[p][:, kq * 128:(kq + 1) * 128],
                            slabP[:, kq * RB:(kq + 1) * RB],
                            start=(kq == 0), stop=(kq == NKT - 1)),
                            waits=tok_tab[p] + [tok_P_last, tok_psf[p]]
                            if kq == 0 else [])
                    tok_tabu[p] = mm
                    ec = ve(lambda e, p=p: e.tensor_copy(stgcv[p], prv[p]),
                            waits=[mm, tok_stgc_free[p]])
                    tok_psf[p] = ec
                    ttr = None
                    for rq in range(RTB):
                        ttr = pe(lambda e, rq=rq, p=p: e.transpose(
                            ptrv[p][:, rq * 128:(rq + 1) * 128],
                            stgcv[p][:, rq * 128:(rq + 1) * 128], s_idb[:]),
                            waits=[ec, tok_ptr_free[p]] if rq == 0 else [ec])
                    tok_stgc_free[p] = ttr
                    ev = None
                    for rq in range(RTB):
                        ev = ve(lambda e, rq=rq, tt_=tt_, p=p: e.tensor_copy(
                            ctok[:, rq * KHP + tt_ * 128:
                                 rq * KHP + (tt_ + 1) * 128],
                            ptrv[p][:, rq * 128:(rq + 1) * 128]), waits=[ttr])
                    tok_ptr_free[p] = ev
                    tok_ct_last = ev
                    if b == 0:
                        drain(lgen, 1)
                    else:
                        drain(pend_asm, 1)

                if pend_asm is not None:
                    while not drain(pend_asm, 4):
                        pass
                    emit_zturn(pend_b, 1, pend_az[1])
                    pend_asm = None
                if b == 0:
                    while not drain(lgen, 4):
                        pass
                    tok_acomb = lt["acomb"]
                    tok_bcomb = lt["bcomb"]
                    tok_lora = [tok_acomb, tok_bcomb]
                    ast["C"] = [tok_acomb]

                tok_et_last = None
                t6_last = None
                for rq in range(RTB):
                    cv = ctok[:, rq * KHP:(rq + 1) * KHP]
                    evv = etok[:, rq * KHP:(rq + 1) * KHP]
                    t1_ = ve(lambda e, cv=cv: e.max(m8ab[:, 0:8], cv[:, 0:KH]),
                             waits=[tok_ct_last, tok_tabu[0], tok_tabu[1]])
                    t2_ = ve(lambda e, cv=cv: e.max(m8ab[:, 8:16], cv[:, 1:2048]),
                             waits=[t1_])
                    t3_ = ve(lambda e, rq=rq: e.max(
                        m8all[:, rq * 8:(rq + 1) * 8], m8ab[:]), waits=[t2_])
                    c0 = m8all[:, rq * 8:rq * 8 + 1]
                    th = m8all[:, rq * 8 + 7:rq * 8 + 8]
                    t4_ = ve(lambda e, c0=c0, rq=rq: e.tensor_scalar(
                        mcol[:, rq:rq + 1], c0, -1.0, None, op0=ALU.mult),
                        waits=[t3_])
                    t5_ = ve(lambda e, cv=cv, th=th: e.tensor_scalar(
                        stgm[:, 0:KHP], cv, th, None, op0=ALU.is_ge),
                        waits=[t3_])
                    t6_ = se(lambda e, cv=cv, rq=rq: e.activation(
                        stgm[:, KHP:2 * KHP], cv, AF.Exp,
                        bias=mcol[:, rq:rq + 1]), waits=[t4_])
                    t6_last = t6_
                    tok_et_last = ve(lambda e, evv=evv: e.tensor_tensor(
                        evv, stgm[:, KHP:2 * KHP], stgm[:, 0:KHP],
                        op=ALU.mult), waits=[t6_, t5_]
                        + (tok_allstats if b == 0 else []))

                efm = ctok
                tok_efm = None
                for dt_ in range(NKT):
                    p = dt_ % 2
                    ttr = None
                    for rq in range(RTB):
                        ttr = pe(lambda e, rq=rq, dt_=dt_, p=p: e.transpose(
                            ptrv[p][:, rq * 128:(rq + 1) * 128],
                            etok[:, rq * KHP + dt_ * 128:
                                 rq * KHP + (dt_ + 1) * 128], s_idb[:]),
                            waits=[tok_et_last, tok_ptr_free[p]] if rq == 0 else [])
                    ev = None
                    for rq in range(RTB):
                        ev = ve(lambda e, rq=rq, dt_=dt_, p=p: e.tensor_copy(
                            efm[:, dt_ * RB + rq * 128:
                                dt_ * RB + (rq + 1) * 128],
                            ptrv[p][:, rq * 128:(rq + 1) * 128]), waits=[ttr])
                    tok_ptr_free[p] = ev
                    tok_efm = ev

                tok_tab = [None, None]
                tok_v_last = None
                for kt in range(NKT):
                    p = kt % 2
                    tok_tab[p] = tab2("dtab%d" % p, tabAv[p][:, 0:KHP],
                                      d_t3c[kt].rearrange("p q k -> p (q k)"),
                                      waits=[tok_tabu[p]])
                    mm = None
                    for dq in range(NKT):
                        mm = pe(lambda e, dq=dq, p=p: e.matmul(
                            prv[p], tabAv[p][:, dq * 128:(dq + 1) * 128],
                            efm[:, dq * RB:(dq + 1) * RB],
                            start=(dq == 0), stop=(dq == NKT - 1)),
                            waits=tok_tab[p] + [tok_efm, tok_psf[p]]
                            if dq == 0 else [])
                    tok_tabu[p] = mm
                    sl = slice(kt * RB, (kt + 1) * RB)
                    x1 = ve(lambda e, sl=sl, p=p: e.tensor_tensor(
                        v_r[:, sl], prv[p], u_r[:, sl], op=ALU.mult), waits=[mm])
                    x2 = ve(lambda e, sl=sl, p=p: e.tensor_tensor(
                        v_i[:, sl], prv[p], u_i[:, sl], op=ALU.mult), waits=[x1])
                    tok_psf[p] = x2
                    tok_v_last = x2

                tok_tab = [None, None]
                tok_vr_last = None
                tok_psB4 = [t_rb, t_rb]
                t4stg = [[etok[:, (3 * p + j) * RB:(3 * p + j + 1) * RB]
                          for j in range(3)] for p in range(2)]
                tok_t4stg = [None, None]
                for tt_ in range(NKT):
                    p = tt_ % 2
                    tok_tab[p] = tab2("dtab%d" % p, tabAv[p][:, 0:KHP],
                                      d_t4c[tt_].rearrange("p q k -> p (q k)"),
                                      waits=[tok_tabu[p]])
                    tokb = tab2("dtabS%d" % p, tabB4[p],
                                d_t4s[tt_].rearrange("p q k -> p (q k)"),
                                waits=[tok_tabu[p],
                                       tok_et_last if tt_ < 2 else None,
                                       tok_sinuse if tt_ < 2 else None])
                    mm = None
                    for kq in range(NKT):
                        mm = pe(lambda e, kq=kq, p=p: e.matmul(
                            prv[p], tabAv[p][:, kq * 128:(kq + 1) * 128],
                            v_r[:, kq * RB:(kq + 1) * RB],
                            start=(kq == 0), stop=(kq == NKT - 1)),
                            waits=tok_tab[p] + [tok_v_last, tok_psf[p]]
                            if kq == 0 else [])
                    mmb = None
                    for kq in range(NKT):
                        mmb = pe(lambda e, kq=kq, p=p: e.matmul(
                            piv[p], tabB4[p][:, kq * 128:(kq + 1) * 128],
                            v_i[:, kq * RB:(kq + 1) * RB],
                            start=(kq == 0), stop=(kq == NKT - 1)),
                            waits=tokb + [tok_psf[p]] if kq == 0 else [])
                    tok_tabu[p] = mmb
                    vre, vpl, vmn = t4stg[p]
                    ec = ve(lambda e, p=p, vre=vre: e.tensor_copy(vre, prv[p]),
                            waits=[mm, tok_t4stg[p],
                                   tok_efm if tt_ < 2 else None])
                    epl = ve(lambda e, p=p, vre=vre, vpl=vpl: e.tensor_tensor(
                        vpl, vre, piv[p], op=ALU.add), waits=[ec, mmb])
                    emn = ve(lambda e, p=p, vre=vre, vmn=vmn: e.tensor_tensor(
                        vmn, vre, piv[p], op=ALU.subtract), waits=[epl])
                    tok_psf[p] = emn
                    ttr = None
                    for rq in range(RTB):
                        ttr = pe(lambda e, rq=rq, p=p, vpl=vpl: e.transpose(
                            ptrv[p][:, rq * 128:(rq + 1) * 128],
                            vpl[:, rq * 128:(rq + 1) * 128], s_idb[:]),
                            waits=[epl, tok_ptr_free[p]] if rq == 0 else [epl])
                    ev = None
                    ncols = 128 if tt_ < NKT - 1 else 1
                    for rq in range(RTB):
                        ev = ve(lambda e, rq=rq, tt_=tt_, p=p, nc_=ncols:
                                e.tensor_copy(
                            vrtok[:, rq * L + tt_ * 128:
                                  rq * L + tt_ * 128 + nc_],
                            ptrv[p][:, rq * 128:rq * 128 + nc_]),
                            waits=[ttr] + (list(ast["vr"]) + tok_lora
                                           if tt_ == 0 else []))
                    tok_ptr_free[p] = ev
                    tok_vr_last = ev
                    if tt_ < NKT - 1:
                        tjr = None
                        for rq in range(RTB):
                            tjr = pe(lambda e, rq=rq, vmn=vmn: e.matmul(
                                psB[:, rq * 128:(rq + 1) * 128],
                                vmn[:, rq * 128:(rq + 1) * 128], s_rev[:],
                                start=True, stop=True),
                                waits=[emn, tok_psB4[0]] if rq == 0 else [emn])
                        lo = 3969 - 128 * tt_
                        ncm = 127 if tt_ == 0 else 128
                        evm = None
                        for rq in range(RTB):
                            evm = ve(lambda e, rq=rq, lo=lo, ncm=ncm:
                                     e.tensor_copy(
                                vrtok[:, rq * L + lo:rq * L + lo + ncm],
                                psB[:, rq * 128:rq * 128 + ncm]),
                                waits=[tjr] + (list(ast["vr"]) + tok_lora
                                               if tt_ == 0 else []))
                        tok_psB4 = [evm, evm]
                        tok_t4stg[p] = evm
                    else:
                        tok_t4stg[p] = ev

                if os.environ.get("K_TRUNC"):
                    rec.step("sync", None, waits=rec.barrier())
                    rec.disabled = True
                    tok_prev_block = [None, None, None, None]
                    break
                prev_entry = [tok_tabu[0], tok_tabu[1], tok_psf[0], tok_psf[1],
                              tok_u_last, tok_et_last]
                tok_psB_carry = tok_psB4[0]
                pend_az = [None, None]
                pend_asm = asm_gen(b, tok_vr_last, pend_az)
                pend_b = b

            if pend_asm is not None:
                while pend_az[0] is None:
                    if drain(pend_asm, 1):
                        break
                emit_zturn(pend_b, 0, pend_az[0])
                while not drain(pend_asm, 4):
                    pass
                emit_zturn(pend_b, 1, pend_az[1])
                pend_asm = None
            if tok_zwr:
                tok_prev_block = [ast["zt"], ast["A"],
                                  tok_zwr[-1], tok_ywr[-1]]
            tok_asmC_fin = list(ast["C"])

        with nc.psum_tensor("pax", [64, 1024], F32) as pax:
            tz = [None] * 4
            tzu = [None] * 4
            mm = None
            for q in range(NQ):
                p = q % 4
                tz[p] = dma("sync", "dz%d" % p, zv2[p], d_zscr[q],
                            waits=[tzu[p]] +
                            (tok_zwr + tok_prev_block if q < 4 else []))
                for h in range(2):
                    mm = pe(lambda e, q=q, p=p, h=h: e.matmul(
                        pax[:, h * 512:(h + 1) * 512],
                        a_comb[:, q * 64:(q + 1) * 64],
                        zv2[p][:, h * 512:(h + 1) * 512],
                        start=(q == 0), stop=(q == NQ - 1)),
                        waits=[tz[p], tok_acomb] + tok_prev_block
                        if q == 0 and h == 0 else ([tz[p]] if h == 0 else []))
                tzu[p] = mm
            tok_ax = ve(lambda e: e.tensor_copy(s_ax[:], pax[:]),
                        waits=[mm, tok_prev_block[1]])

        tok_bias = dma("gpsimd", "dbias", biasbf[:], d_bias,
                       waits=[tok_ax] + tok_prev_block)
        tok_bb = tok_bias

        with ExitStack() as pctx:
            pf = [pctx.enter_context(nc.psum_tensor("pf%d" % i, [128, 512], F32))
                  for i in range(8)]
            twu = [None] * 4
            tz2 = [None] * 4
            tz2u = [None] * 4
            tok_fst = []
            tok_pf_free = [tok_bb] * 8
            fsv = [asmB[:, j * 512:(j + 1) * 512] for j in range(4)]
            tok_fsv = [tok_prev_block[1]] * 4
            tok_s2prev = tok_asmC_fin[3] if tok_asmC_fin else None
            NWCH = 8 * NQ
            wstg = [asmA[:, s * 512:(s + 1) * 512] for s in range(8)]
            tw_ld = [None] * NWCH
            tw_cast = [None] * NWCH
            w_init = [tok_ax, tok_bb] + tok_prev_block

            ORDER = []
            for ot_ in range(8):
                ks = range(NQ) if ot_ % 2 == 0 else range(NQ - 1, -1, -1)
                for kq_ in ks:
                    ORDER.append((ot_, kq_))

            def w_load(idx):
                s = idx % 8
                ot_, kq_ = ORDER[idx]
                tw_ld[idx] = dma(
                    "scalar", "dwh%d" % s, wstg[s],
                    d_w[kq_ * 128:(kq_ + 1) * 128,
                        ot_ * 512:(ot_ + 1) * 512],
                    waits=[tw_cast[idx - 8]] if idx >= 8 else w_init)

            PRE = 6
            for i in range(PRE):
                w_load(i)
            cur_z = [None] * 4
            idx = 0
            for ot in range(8):
                ks = range(NQ) if ot % 2 == 0 else range(NQ - 1, -1, -1)
                mm_rt = [None] * 8
                for i, kq in enumerate(ks):
                    p = kq % 4
                    if idx + PRE < NWCH:
                        w_load(idx + PRE)
                    tw_cast[idx] = se(
                        lambda e, p=p, s=idx % 8: e.activation(
                            wv[p], wstg[s], AF.Copy),
                        waits=[tw_ld[idx],
                               twu[p] if twu[p] else tok_prev_block[0]])
                    if cur_z[p] != kq:
                        tz2[p] = dma("sync", "dz%d" % p, zv2[p], d_zscr[kq],
                                     waits=[tz2u[p],
                                            tok_ax if (ot == 0 and i < 4)
                                            else None]
                                     + (tok_zwr if ot == 0 and i < 4 else []))
                        cur_z[p] = kq
                    mm = None
                    for rt in range(8):
                        w0 = []
                        if rt == 0:
                            w0 = [tw_cast[idx], tz2[p]]
                            if i == 0:
                                w0 += [tok_ax, tok_bb]
                        if i == 0:
                            w0 = w0 + [tok_pf_free[rt]]
                        mm = pe(lambda e, rt=rt, p=p, st=(i == 0): e.matmul(
                            pf[rt][:], zv2[p][:, rt * 128:(rt + 1) * 128],
                            wv[p], start=st, stop=False), waits=w0)
                        mm_rt[rt] = mm
                    twu[p] = mm
                    tz2u[p] = mm
                    idx += 1
                for rt in range(8):
                    mm_rt[rt] = pe(lambda e, rt=rt, ot=ot: e.matmul(
                        pf[rt][:], s_ax[:, rt * 128:(rt + 1) * 128],
                        b_comb[:, ot * 512:(ot + 1) * 512],
                        start=False, stop=False),
                        waits=[tok_ax, tok_bcomb] if rt == 0 else [])
                for rt in range(8):
                    mm_rt[rt] = pe(lambda e, rt=rt, ot=ot: e.matmul(
                        pf[rt][:], s_onb[:],
                        biasbf[0:1, ot * 512:(ot + 1) * 512],
                        start=False, stop=True),
                        waits=[tok_bias, tok_misc] if rt == 0 else [])
                for rt in range(8):
                    j = (ot * 8 + rt) % 4
                    fb = fsv[j]
                    e1 = ve(lambda e, rt=rt, fb=fb: e.tensor_copy(
                        fb, pf[rt][:]),
                        waits=[mm_rt[rt], tok_fsv[j]])
                    tok_pf_free[rt] = e1
                    s1 = ve(lambda e, rt=rt, ot=ot, fb=fb: e.reduce_sum(
                        fsum[:, rt * 8 + ot:rt * 8 + ot + 1], fb,
                        axis=AX.X), waits=[e1])
                    s2 = se(lambda e, rt=rt, ot=ot, fb=fb: e.activation(
                        tr512, fb, AF.Square,
                        accum_out=fsq[:, rt * 8 + ot:rt * 8 + ot + 1]),
                        waits=[e1, s1, tok_s2prev])
                    tok_s2prev = s2
                    tok_fw = dma("sync", "dfw%d" % j,
                                 d_fscr[rt][:, ot * 512:(ot + 1) * 512],
                                 fb, waits=[s1, s2])
                    tok_fst.append((s1, s2, tok_fw))
                    tok_fsv[j] = tok_fw

            tok_fstats = [t for tpl in tok_fst for t in tpl[:2]]
            tok_fwr = [tpl[2] for tpl in tok_fst]

        LH2 = L // 2
        tok_out = []
        tok_tf = [None, None]
        tok_ty = [None, None]
        tok_p1 = [None, None]
        tok_p2 = [None, None]
        tok_do = [None, None]
        tst4s = []
        for rt in range(8):
            b8 = rt * 8
            c1 = ve(lambda e, rt=rt, o=stat4[:, b8:b8 + 1]: e.reduce_sum(
                o, fsum[:, rt * 8:(rt + 1) * 8], axis=AX.X),
                waits=tok_fstats if rt == 0 else [])
            c2 = ve(lambda e, rt=rt, o=stat4[:, b8 + 1:b8 + 2]: e.reduce_sum(
                o, fsq[:, rt * 8:(rt + 1) * 8], axis=AX.X), waits=[c1])
            tst4s.append(ln_stats(stat4, b8, c1, c2))
        for rt in range(8):
            b8 = rt * 8
            for h in range(2):
                fh = asmA[:, h * LH2:(h + 1) * LH2]
                yh = asmB[:, h * LH2:(h + 1) * LH2]
                oh = asmC[:, h * LH2:(h + 1) * LH2]
                tf = dma("sync", "dfr", fh,
                         d_fscr[rt][:, h * LH2:(h + 1) * LH2],
                         waits=[tok_do[h], tw_cast[NWCH - 1] if rt == 0 else None]
                         + (tok_fwr if rt == 0 else []))
                ty = dma("sync", "dyr", yh,
                         d_yscr[rt][:, h * LH2:(h + 1) * LH2],
                         waits=[tok_p2[h]]
                         + (tok_ywr + tok_fwr if rt == 0 else []))
                p1 = ve(lambda e, oh=oh, fh=fh, m4=stat4[:, b8 + 2:b8 + 3],
                        r4=stat4[:, b8 + 5:b8 + 6]: e.tensor_scalar(
                    oh, fh, m4, r4, op0=ALU.subtract, op1=ALU.mult),
                    waits=[tf, tst4s[rt], tok_p2[h]]
                    + (tok_asmC_fin if rt == 0 else []))
                p2 = ve(lambda e, fh=fh, oh=oh, yh=yh: e.tensor_tensor(
                    fh, oh, yh, op=ALU.add), waits=[p1, ty])
                tok_do[h] = dma("sync", "dout%d" % h,
                                d_out[rt * 128:(rt + 1) * 128,
                                      h * LH2:(h + 1) * LH2],
                                fh, waits=[p2])
                tok_out.append(tok_do[h])
                tok_tf[h] = tf
                tok_ty[h] = ty
                tok_p1[h] = p1
                tok_p2[h] = p2
        rec.step("sync", None, waits=tok_out)

        with nc.Block() as block:
            def mk(eng_name):
                def emit(eng):
                    for (e_, fn, waits, inc, n) in rec.steps:
                        if e_ != eng_name:
                            continue
                        for s, v in waits:
                            eng.wait_ge(sems[s], v)
                        if fn is None:
                            continue
                        inst = fn(eng)
                        if inc is not None:
                            inst.then_inc(sems[inc], n)
                return emit

            block.sync(mk("sync"))
            block.gpsimd(mk("gpsimd"))
            block.tensor(mk("tensor"))
            block.vector(mk("vector"))
            block.scalar(mk("scalar"))

    globals()["_LAST_REC"] = rec
    return nc


_NC_CACHE = None


def kernel(L0_attn_x, L0_ffn_gate, L0_ffn_kernel, L0_ffn_bias, L0_ffn_lora_A,
           L0_ffn_lora_B, L0_ffn_lora_A_shared, L0_ffn_lora_B_shared):
    global _NC_CACHE
    x = np.ascontiguousarray(np.asarray(L0_attn_x, dtype=np.float32).reshape(8192, L))
    gate = np.asarray(L0_ffn_gate, dtype=np.float32)
    w = np.ascontiguousarray(np.asarray(L0_ffn_kernel, dtype=np.float32))
    bias = np.asarray(L0_ffn_bias, dtype=np.float32).reshape(1, L)
    lora_A = np.asarray(L0_ffn_lora_A, dtype=np.float32)
    lora_B = np.asarray(L0_ffn_lora_B, dtype=np.float32)
    A_sh = np.asarray(L0_ffn_lora_A_shared, dtype=np.float32)
    B_sh = np.asarray(L0_ffn_lora_B_shared, dtype=np.float32)

    tb = _make_tables()

    lA = np.ascontiguousarray(
        lora_A.transpose(0, 2, 1).reshape(16, NQ, 128, 32)
        .transpose(0, 2, 1, 3).reshape(16, 128, 1024))
    lAs = np.ascontiguousarray(
        A_sh.transpose(1, 0).reshape(NQ, 128, 32).transpose(1, 0, 2)
        .reshape(128, 1024))
    lB = np.ascontiguousarray(lora_B.transpose(0, 2, 1))
    lBs = np.ascontiguousarray(B_sh.transpose(1, 0))

    if _NC_CACHE is None:
        _NC_CACHE = build_program()
    nc = _NC_CACHE

    in_maps = []
    for c in range(NCORE):
        shard = np.ascontiguousarray(x[c * R:(c + 1) * R])
        shard_T = np.ascontiguousarray(shard.T)
        shard_Tr = np.ascontiguousarray(np.roll(shard_T[::-1, :], 1, axis=0))
        gb = np.ascontiguousarray(np.tile(gate[c:c + 1, :], (128, 1)))
        in_maps.append({
            "x": shard, "xT": shard_T, "xTr": shard_Tr, "w": w, "biasr": bias, "gateb": gb,
            "lA": lA, "lAs": lAs, "lB": lB, "lBs": lBs,
            "t1c": tb["t1c"], "t1s": tb["t1s"], "t2c": tb["t2c"],
            "t3c": tb["t3c"], "t4c": tb["t4c"], "t4s": tb["t4s"],
            "identb": tb["identb"], "revb": tb["revb"],
            "identf": tb["identf"],
            "onesb": tb["onesb"], "onesf": tb["onesf"],
        })

    res = run_bass_kernel_spmd(nc, in_maps, core_ids=list(range(NCORE)))
    out = np.concatenate([np.asarray(r["out"]) for r in res.results], axis=0)
    return out.reshape(2, 4096, L).astype(np.float32)
